# version 02 (final)
# baseline (speedup 1.0000x reference)
import numpy as np
import ml_dtypes
from contextlib import ExitStack
import concourse.bass as bass
import concourse.mybir as mybir
from concourse.bass_utils import run_bass_kernel_spmd

F32 = mybir.dt.float32
BF16 = mybir.dt.bfloat16
AF = mybir.ActivationFunctionType
ALU = mybir.AluOpType
AX = mybir.AxisListType

NC = 8
T = 16384
TL = T // NC
D = 1024
NT = TL // 128
N4 = TL // 512
MAGIC = float(np.float32(1.5 * 2 ** 23))
ENGS = ("sync", "vector", "scalar", "tensor", "gpsimd")
CORE_IDS = list(range(NC))


class Seq:

    def __init__(self, nc):
        self.nc = nc
        self.ops = []

    def op(self, eng, fn):
        self.ops.append((eng, fn, "c"))

    def dma(self, eng, fn):
        self.ops.append((eng, fn, "d"))

    def build(self, block, gsem, dsem):
        n_comp = sum(1 for _, _, k in self.ops if k == "c")
        n_dma = sum(1 for _, _, k in self.ops if k == "d")
        pre = []
        a, b = 0, 0
        for _, _, k in self.ops:
            pre.append((a, b))
            if k == "c":
                a += 1
            else:
                b += 1
        per_eng = {e: [(i, fn, k) for i, (e2, fn, k) in enumerate(self.ops) if e2 == e]
                   for e in ENGS}

        def make_body(ename):
            items = per_eng[ename]

            def body(eng):
                have_c, have_d = 0, 0
                for i, fn, k in items:
                    need_c, need_d = pre[i]
                    if need_c > have_c:
                        eng.wait_ge(gsem, need_c)
                        have_c = need_c
                    if need_d > have_d:
                        eng.wait_ge(dsem, 16 * need_d)
                        have_d = need_d
                    ins = fn(eng)
                    if k == "c":
                        ins.then_inc(gsem, 1)
                        have_c += 0
                    else:
                        ins.then_inc(dsem, 16)
                eng.wait_ge(gsem, n_comp)
                eng.wait_ge(dsem, 16 * n_dma)
            return body

        for ename in ENGS:
            getattr(block, ename)(make_body(ename))


def bind(fn, **kw):
    return lambda eng: fn(eng, **kw)


def host_cols(v):
    v = np.asarray(v, np.float32)
    return np.ascontiguousarray(v.reshape(-1, 128).T)


def host_row(v):
    return np.ascontiguousarray(np.asarray(v, np.float32).reshape(1, -1))



def build_A1():
    nc = bass.Bass("TRN2", target_bir_lowering=False, debug=False)
    x = nc.dram_tensor("x", [TL, D], F32, kind="ExternalInput").ap()
    wq = nc.dram_tensor("wq", [D, 512], F32, kind="ExternalInput").ap()
    wk = nc.dram_tensor("wk", [D, 512], F32, kind="ExternalInput").ap()
    wv = nc.dram_tensor("wv", [D, 1024], F32, kind="ExternalInput").ap()
    wg = nc.dram_tensor("wg", [D, 1024], F32, kind="ExternalInput").ap()
    wgk1 = nc.dram_tensor("wgk1", [D, 16], F32, kind="ExternalInput").ap()
    wgk2 = nc.dram_tensor("wgk2", [16, 512], F32, kind="ExternalInput").ap()
    bgk2c = nc.dram_tensor("bgk2c", [128, 4], F32, kind="ExternalInput").ap()
    bgk2r = nc.dram_tensor("bgk2r", [1, 512], F32, kind="ExternalInput").ap()
    anw = nc.dram_tensor("anw", [128, 8], F32, kind="ExternalInput").ap()

    o_ln = nc.dram_tensor("o_ln", [TL, D], F32, kind="ExternalOutput").ap()
    o_qfm = nc.dram_tensor("o_qfm", [512, TL], BF16, kind="ExternalOutput").ap()
    o_kfm = nc.dram_tensor("o_kfm", [512, TL], BF16, kind="ExternalOutput").ap()
    o_ktm = nc.dram_tensor("o_ktm", [TL, 512], BF16, kind="ExternalOutput").ap()
    o_vtm = nc.dram_tensor("o_vtm", [TL, 1024], BF16, kind="ExternalOutput").ap()
    o_glfm = nc.dram_tensor("o_glfm", [512, TL], F32, kind="ExternalOutput").ap()
    o_gltm = nc.dram_tensor("o_gltm", [TL, 512], F32, kind="ExternalOutput").ap()
    o_gout = nc.dram_tensor("o_gout", [1024, TL], F32, kind="ExternalOutput").ap()

    s = Seq(nc)
    with ExitStack() as st:
        T_ = st.enter_context
        xt = T_(nc.sbuf_tensor("xt", [128, D], F32))
        lnt = T_(nc.sbuf_tensor("lnt", [128, D], F32))
        sq = T_(nc.sbuf_tensor("sq", [128, D], F32))
        htm = T_(nc.sbuf_tensor("htm", [128, NT * D], BF16))
        hfm = T_(nc.sbuf_tensor("hfm", [128, 8 * TL], BF16))
        cols = T_(nc.sbuf_tensor("cols", [128, 8], F32))
        epsc = T_(nc.sbuf_tensor("epsc", [128, 2], F32))
        anwt = T_(nc.sbuf_tensor("anwt", [128, 8], F32))
        bgct = T_(nc.sbuf_tensor("bgct", [128, 4], F32))
        bgrt = T_(nc.sbuf_tensor("bgrt", [1, 512], F32))
        bgrb = T_(nc.sbuf_tensor("bgrb", [1, 512], BF16))
        ones1 = T_(nc.sbuf_tensor("ones1", [1, 128], BF16))
        wload = T_(nc.sbuf_tensor("wload", [128, 1024], F32))
        wqb = T_(nc.sbuf_tensor("wqb", [128, 8 * 512], BF16))
        wkb = T_(nc.sbuf_tensor("wkb", [128, 8 * 512], BF16))
        wvb = T_(nc.sbuf_tensor("wvb", [128, 8 * 1024], BF16))
        wgb = T_(nc.sbuf_tensor("wgb", [128, 8 * 1024], BF16))
        wg1b = T_(nc.sbuf_tensor("wg1b", [128, 8 * 16], BF16))
        wg2b = T_(nc.sbuf_tensor("wg2b", [16, 512], BF16))
        z1b = T_(nc.sbuf_tensor("z1b", [16, TL], BF16))
        stg = T_(nc.sbuf_tensor("stg", [128, 512], F32))
        stgb = T_(nc.sbuf_tensor("stgb", [128, 512], BF16))
        ta = T_(nc.sbuf_tensor("ta", [128, 512], F32))
        tb = T_(nc.sbuf_tensor("tb", [128, 512], F32))
        tcc = T_(nc.sbuf_tensor("tcc", [128, 512], F32))
        ps0 = T_(nc.psum_tensor("ps0", [128, 512], F32))
        ps1 = T_(nc.psum_tensor("ps1", [128, 512], F32))
        psz = T_(nc.psum_tensor("psz", [16, 512], F32))
        gsem = T_(nc.semaphore("gsem"))
        dsem = T_(nc.semaphore("dsem"))
        block = T_(nc.Block())

        s.op("vector", lambda e: e.memset(ones1[:, :], 1.0))
        s.op("vector", lambda e: e.memset(epsc[:, 0:1], 1e-5))
        s.op("vector", lambda e: e.memset(epsc[:, 1:2], 1e-6))
        s.dma("sync", lambda e: e.dma_start(out=anwt[:, :], in_=anw))
        s.dma("sync", lambda e: e.dma_start(out=bgct[:, :], in_=bgk2c))
        s.dma("sync", lambda e: e.dma_start(out=bgrt[:, :], in_=bgk2r))
        s.op("vector", lambda e: e.tensor_copy(out=bgrb[:, :], in_=bgrt[:, :]))

        for (w_ap, dstt, outw) in ((wq, wqb, 512), (wk, wkb, 512), (wv, wvb, 1024),
                                   (wg, wgb, 1024), (wgk1, wg1b, 16)):
            for c in range(8):
                s.dma("sync", bind(lambda e, w_ap=w_ap, c=c, outw=outw:
                                   e.dma_start(out=wload[:, 0:outw],
                                               in_=w_ap[c * 128:(c + 1) * 128, :])))
                s.op("vector", bind(lambda e, dstt=dstt, c=c, outw=outw:
                                    e.tensor_scalar(out=dstt[:, c * outw:(c + 1) * outw],
                                                    in0=wload[:, 0:outw],
                                                    scalar1=anwt[:, c:c + 1], scalar2=None,
                                                    op0=ALU.mult)))
        s.dma("sync", lambda e: e.dma_start(out=wload[0:16, 0:512], in_=wgk2))
        s.op("vector", lambda e: e.tensor_copy(out=wg2b[:, :], in_=wload[0:16, 0:512]))

        for i in range(NT):
            s.dma("sync", bind(lambda e, i=i: e.dma_start(out=xt[:, :],
                                                          in_=x[i * 128:(i + 1) * 128, :])))
            s.op("scalar", lambda e: e.activation(out=sq[:, :], in_=xt[:, :], func=AF.Square,
                                                  accum_out=cols[:, 0:1]))
            s.op("vector", lambda e: e.reduce_sum(out=cols[:, 1:2], in_=xt[:, :], axis=AX.X))
            s.op("vector", lambda e: e.tensor_scalar(out=cols[:, 2:3], in0=cols[:, 1:2],
                                                     scalar1=1.0 / D, scalar2=None, op0=ALU.mult))
            s.op("vector", lambda e: e.tensor_scalar(out=cols[:, 3:4], in0=cols[:, 0:1],
                                                     scalar1=1.0 / D, scalar2=None, op0=ALU.mult))
            s.op("vector", lambda e: e.scalar_tensor_tensor(out=cols[:, 4:5], in0=cols[:, 2:3],
                                                            scalar=cols[:, 2:3], in1=cols[:, 3:4],
                                                            op0=ALU.mult, op1=ALU.subtract))
            s.op("scalar", lambda e: e.activation(out=cols[:, 5:6], in_=cols[:, 4:5], func=AF.Sqrt,
                                                  scale=-1.0, bias=epsc[:, 0:1]))
            s.op("vector", lambda e: e.reciprocal(out=cols[:, 6:7], in_=cols[:, 5:6]))
            s.op("vector", lambda e: e.tensor_scalar(out=lnt[:, :], in0=xt[:, :],
                                                     scalar1=cols[:, 2:3], scalar2=cols[:, 6:7],
                                                     op0=ALU.subtract, op1=ALU.mult))
            s.dma("gpsimd", bind(lambda e, i=i: e.dma_start(out=o_ln[i * 128:(i + 1) * 128, :],
                                                            in_=lnt[:, :])))
            s.op("scalar", lambda e: e.activation(out=sq[:, :], in_=lnt[:, :], func=AF.Square,
                                                  accum_out=cols[:, 0:1]))
            s.op("vector", lambda e: e.tensor_scalar(out=cols[:, 1:2], in0=cols[:, 0:1],
                                                     scalar1=1.0 / D, scalar2=None, op0=ALU.mult))
            s.op("scalar", lambda e: e.activation(out=cols[:, 2:3], in_=cols[:, 1:2], func=AF.Sqrt,
                                                  bias=epsc[:, 1:2]))
            s.op("vector", lambda e: e.reciprocal(out=cols[:, 3:4], in_=cols[:, 2:3]))
            s.op("vector", bind(lambda e, i=i: e.tensor_scalar(out=htm[:, i * D:(i + 1) * D],
                                                               in0=lnt[:, :], scalar1=cols[:, 3:4],
                                                               scalar2=None, op0=ALU.mult)))

        for i in range(NT):
            for j in range(8):
                s.dma("sync", bind(lambda e, i=i, j=j: e.dma_start_transpose(
                    out=hfm[:, j * TL + i * 128: j * TL + (i + 1) * 128],
                    in_=htm[:, i * D + j * 128: i * D + (j + 1) * 128])))

        def proj_fm(wtiles, outw, evac):
            for oc in range(outw // 128):
                for tt in range(N4):
                    for c in range(8):
                        s.op("tensor", bind(lambda e, oc=oc, tt=tt, c=c:
                                            e.matmul(out=ps0[:, :],
                                                     lhsT=wtiles[:, c * outw + oc * 128:
                                                                 c * outw + (oc + 1) * 128],
                                                     rhs=hfm[:, c * TL + tt * 512:
                                                             c * TL + (tt + 1) * 512],
                                                     start=(c == 0), stop=(c == 7))))
                    evac(oc, tt)

        def evac_to(dst, stage, f32=False):
            def f(oc, tt):
                s.op("vector", lambda e: e.tensor_copy(out=stage[:, :], in_=ps0[:, :]))
                s.dma("gpsimd", bind(lambda e, oc=oc, tt=tt: e.dma_start(
                    out=dst[oc * 128:(oc + 1) * 128, tt * 512:(tt + 1) * 512],
                    in_=stage[:, :])))
            return f

        proj_fm(wqb, 512, evac_to(o_qfm, stgb))
        proj_fm(wkb, 512, evac_to(o_kfm, stgb))
        proj_fm(wgb, 1024, evac_to(o_gout, stg))

        def proj_tm(wtiles, outw, dst):
            for i in range(NT):
                for oc in range(outw // 512):
                    for c in range(8):
                        s.op("tensor", bind(lambda e, i=i, oc=oc, c=c, outw=outw:
                                            e.matmul(out=ps0[:, :],
                                                     lhsT=hfm[:, c * TL + i * 128:
                                                              c * TL + (i + 1) * 128],
                                                     rhs=wtiles[:, c * outw + oc * 512:
                                                                c * outw + (oc + 1) * 512],
                                                     start=(c == 0), stop=(c == 7))))
                    s.op("vector", lambda e: e.tensor_copy(out=stgb[:, :], in_=ps0[:, :]))
                    s.dma("gpsimd", bind(lambda e, i=i, oc=oc: e.dma_start(
                        out=dst[i * 128:(i + 1) * 128, oc * 512:(oc + 1) * 512],
                        in_=stgb[:, :])))

        proj_tm(wkb, 512, o_ktm)
        proj_tm(wvb, 1024, o_vtm)

        for tt in range(N4):
            for c in range(8):
                s.op("tensor", bind(lambda e, tt=tt, c=c:
                                    e.matmul(out=psz[:, :],
                                             lhsT=wg1b[:, c * 16:(c + 1) * 16],
                                             rhs=hfm[:, c * TL + tt * 512: c * TL + (tt + 1) * 512],
                                             start=(c == 0), stop=(c == 7))))
            s.op("vector", bind(lambda e, tt=tt: e.tensor_copy(out=z1b[:, tt * 512:(tt + 1) * 512],
                                                               in_=psz[:, :])))

        def glog_chain(ps, bias_col, out_hbm, r0, c0):
            if bias_col is not None:
                s.op("scalar", lambda e: e.activation(out=ta[:, :], in_=ps, func=AF.Abs,
                                                      bias=bias_col))
                s.op("vector", lambda e: e.tensor_scalar(out=tb[:, :], in0=ps, scalar1=bias_col,
                                                         scalar2=0.0, op0=ALU.add, op1=ALU.min))
            else:
                s.op("scalar", lambda e: e.activation(out=ta[:, :], in_=ps, func=AF.Abs))
                s.op("vector", lambda e: e.tensor_scalar(out=tb[:, :], in0=ps, scalar1=0.0,
                                                         scalar2=None, op0=ALU.min))
            s.op("scalar", lambda e: e.activation(out=ta[:, :], in_=ta[:, :], func=AF.Exp, scale=-1.0))
            s.op("scalar", lambda e: e.activation(out=ta[:, :], in_=ta[:, :], func=AF.Ln, bias=1.0))
            s.op("vector", lambda e: e.tensor_tensor(out=tcc[:, :], in0=tb[:, :], in1=ta[:, :],
                                                     op=ALU.subtract))
            s.op("vector", lambda e: e.tensor_scalar(out=tcc[:, :], in0=tcc[:, :], scalar1=1.0 / 16.0,
                                                     scalar2=None, op0=ALU.mult))
            s.dma("gpsimd", bind(lambda e, r0=r0, c0=c0: e.dma_start(
                out=out_hbm[r0:r0 + 128, c0:c0 + 512], in_=tcc[:, :])))

        for oc in range(4):
            for tt in range(N4):
                s.op("tensor", bind(lambda e, oc=oc, tt=tt: e.matmul(
                    out=ps0[:, :], lhsT=wg2b[:, oc * 128:(oc + 1) * 128],
                    rhs=z1b[:, tt * 512:(tt + 1) * 512], start=True, stop=True)))
                glog_chain(ps0[:, :], bgct[:, oc:oc + 1], o_glfm, oc * 128, tt * 512)

        for i in range(NT):
            s.op("tensor", bind(lambda e, i=i: e.matmul(
                out=ps1[:, :], lhsT=z1b[:, i * 128:(i + 1) * 128], rhs=wg2b[:, :],
                start=True, stop=False)))
            s.op("tensor", lambda e: e.matmul(out=ps1[:, :], lhsT=ones1[0:1, :], rhs=bgrb[0:1, :],
                                              start=False, stop=True))
            glog_chain(ps1[:, :], None, o_gltm, i * 128, 0)

        s.build(block, gsem, dsem)
    return nc


def prep_A1(inp):
    x = np.asarray(inp["L0_attn_hidden_states"], np.float32).reshape(T, D)
    common = {
        "wq": np.asarray(inp["L0_attn_Wq"], np.float32),
        "wk": np.asarray(inp["L0_attn_Wk"], np.float32),
        "wv": np.asarray(inp["L0_attn_Wv"], np.float32),
        "wg": np.asarray(inp["L0_attn_Wg"], np.float32),
        "wgk1": np.asarray(inp["L0_attn_Wgk1"], np.float32),
        "wgk2": np.asarray(inp["L0_attn_Wgk2"], np.float32),
        "bgk2c": host_cols(inp["L0_attn_bgk2"]),
        "bgk2r": host_row(inp["L0_attn_bgk2"]),
        "anw": host_cols(inp["L0_attn_attn_norm_w"]),
    }
    return [dict(common, x=np.ascontiguousarray(x[c * TL:(c + 1) * TL])) for c in range(NC)]


def run_A1(inp):
    nc = build_A1()
    res = run_bass_kernel_spmd(nc, prep_A1(inp), core_ids=CORE_IDS)
    return res.results


def kernel(**inp):
    r1 = run_A1(inp)
    raise NotImplementedError("later launches not yet built")



SEG = 2048
NSEG = T // SEG
SCALE_QK = 0.08838834764831845


def build_A2():
    nc = bass.Bass("TRN2", target_bir_lowering=False, debug=False)
    qfm = nc.dram_tensor("qfm", [128, T], BF16, kind="ExternalInput").ap()
    kfm = nc.dram_tensor("kfm", [128, T], BF16, kind="ExternalInput").ap()
    ktm = nc.dram_tensor("ktm", [T, 128], BF16, kind="ExternalInput").ap()
    vtm = nc.dram_tensor("vtm", [T, 128], BF16, kind="ExternalInput").ap()
    glfm = nc.dram_tensor("glfm", [128, T], F32, kind="ExternalInput").ap()
    gltm = nc.dram_tensor("gltm", [T, 128], F32, kind="ExternalInput").ap()
    l2t = nc.dram_tensor("l2t", [128, 128], F32, kind="ExternalInput").ap()
    maskt = nc.dram_tensor("maskt", [128, 64], F32, kind="ExternalInput").ap()
    rmask = nc.dram_tensor("rmask", [128, SEG], F32, kind="ExternalInput").ap()
    o_o = nc.dram_tensor("o_o", [128, T], F32, kind="ExternalOutput").ap()

    s = Seq(nc)
    with ExitStack() as st:
        T_ = st.enter_context
        qseg = T_(nc.sbuf_tensor("qseg", [128, SEG], BF16))
        kseg = T_(nc.sbuf_tensor("kseg", [128, SEG], BF16))
        glseg = T_(nc.sbuf_tensor("glseg", [128, SEG], F32))
        ktseg = T_(nc.sbuf_tensor("ktseg", [128, SEG], BF16))
        vtseg = T_(nc.sbuf_tensor("vtseg", [128, SEG], BF16))
        gltseg = T_(nc.sbuf_tensor("gltseg", [128, SEG], F32))
        bfm = T_(nc.sbuf_tensor("bfm", [128, SEG], F32))
        ebp = T_(nc.sbuf_tensor("ebp", [128, SEG], F32))
        qe = T_(nc.sbuf_tensor("qe", [128, SEG], BF16))
        kefm = T_(nc.sbuf_tensor("kefm", [128, SEG], BF16))
        ketm = T_(nc.sbuf_tensor("ketm", [128, SEG], BF16))
        ebntm = T_(nc.sbuf_tensor("ebntm", [128, 128], F32))
        oseg = T_(nc.sbuf_tensor("oseg", [128, SEG], F32))
        l2tt = T_(nc.sbuf_tensor("l2tt", [128, 128], F32))
        maskt_t = T_(nc.sbuf_tensor("maskt_t", [128, 64], F32))
        rmaskt = T_(nc.sbuf_tensor("rmaskt", [128, SEG], F32))
        S = T_(nc.sbuf_tensor("S", [128, 128], F32))
        Sbf = T_(nc.sbuf_tensor("Sbf", [128, 128], BF16))
        Ab = T_(nc.sbuf_tensor("Ab", [128, 64], BF16))
        ebl = T_(nc.sbuf_tensor("ebl", [128, 1], F32))
        psA = T_(nc.psum_tensor("psA", [128, 64], F32))
        psO = T_(nc.psum_tensor("psO", [128, 64], F32))
        psS = T_(nc.psum_tensor("psS", [128, 128], F32))
        psB = T_(nc.psum_tensor("psB", [128, 128], F32))
        gsem = T_(nc.semaphore("gsem"))
        dsem = T_(nc.semaphore("dsem"))
        block = T_(nc.Block())

        s.dma("sync", lambda e: e.dma_start(out=l2tt[:, :], in_=l2t))
        s.dma("sync", lambda e: e.dma_start(out=maskt_t[:, :], in_=maskt))
        s.dma("sync", lambda e: e.dma_start(out=rmaskt[:, :], in_=rmask))
        s.op("vector", lambda e: e.memset(S[:, :], 0.0))

        for g in range(NSEG):
            t0 = g * SEG
            s.dma("sync", bind(lambda e, t0=t0: e.dma_start(out=qseg[:, :], in_=qfm[:, t0:t0 + SEG])))
            s.dma("sync", bind(lambda e, t0=t0: e.dma_start(out=kseg[:, :], in_=kfm[:, t0:t0 + SEG])))
            s.dma("sync", bind(lambda e, t0=t0: e.dma_start(out=glseg[:, :], in_=glfm[:, t0:t0 + SEG])))
            s.dma("sync", bind(lambda e, t0=t0: e.dma_start(
                out=ktseg[:, :].rearrange("p (i e) -> p i e", i=SEG // 128),
                in_=ktm[t0:t0 + SEG, :].rearrange("(i p) e -> p i e", p=128))))
            s.dma("sync", bind(lambda e, t0=t0: e.dma_start(
                out=vtseg[:, :].rearrange("p (i e) -> p i e", i=SEG // 128),
                in_=vtm[t0:t0 + SEG, :].rearrange("(i p) e -> p i e", p=128))))
            s.dma("sync", bind(lambda e, t0=t0: e.dma_start(
                out=gltseg[:, :].rearrange("p (i e) -> p i e", i=SEG // 128),
                in_=gltm[t0:t0 + SEG, :].rearrange("(i p) e -> p i e", p=128))))
            s.op("vector", lambda e: e.tensor_tensor_scan(out=bfm[:, :], data0=rmaskt[:, :],
                                                          data1=glseg[:, :], initial=0.0,
                                                          op0=ALU.mult, op1=ALU.add))
            s.op("scalar", lambda e: e.activation(out=ebp[:, :], in_=bfm[:, :], func=AF.Exp))
            s.op("vector", lambda e: e.scalar_tensor_tensor(out=qe[:, :], in0=qseg[:, :],
                                                            scalar=SCALE_QK, in1=ebp[:, :],
                                                            op0=ALU.mult, op1=ALU.mult))
            s.op("scalar", lambda e: e.activation(out=ebp[:, :], in_=bfm[:, :], func=AF.Exp,
                                                  scale=-1.0))
            s.op("vector", lambda e: e.tensor_tensor(out=kefm[:, :], in0=kseg[:, :],
                                                     in1=ebp[:, :], op=ALU.mult))
            for i in range(SEG // 128):
                c0 = i * 128
                s.op("tensor", bind(lambda e, c0=c0: e.matmul(
                    out=psB[:, :], lhsT=l2tt[:, :], rhs=gltseg[:, c0:c0 + 128],
                    start=True, stop=True)))
                s.op("scalar", lambda e: e.activation(out=ebntm[:, :], in_=psB[:, :],
                                                      func=AF.Exp, scale=-1.0))
                s.op("vector", bind(lambda e, c0=c0: e.tensor_tensor(
                    out=ketm[:, c0:c0 + 128], in0=ktseg[:, c0:c0 + 128], in1=ebntm[:, :],
                    op=ALU.mult)))
                for p in range(2):
                    n0 = c0 + p * 64
                    pb = p * 64
                    s.op("tensor", bind(lambda e, n0=n0, pb=pb: e.matmul(
                        out=psA[pb:pb + 64, :], lhsT=kefm[:, n0:n0 + 64], rhs=qe[:, n0:n0 + 64],
                        start=True, stop=True)))
                    s.op("vector", bind(lambda e, pb=pb: e.tensor_tensor(
                        out=Ab[pb:pb + 64, :], in0=psA[pb:pb + 64, :],
                        in1=maskt_t[pb:pb + 64, :], op=ALU.mult)))
                    s.op("vector", lambda e: e.tensor_copy(out=Sbf[:, :], in_=S[:, :]))
                    s.op("tensor", bind(lambda e, c0=c0, pb=pb: e.matmul(
                        out=psO[:, :], lhsT=vtseg[pb:pb + 64, c0:c0 + 128],
                        rhs=Ab[pb:pb + 64, :], start=True, stop=False)))
                    s.op("tensor", bind(lambda e, n0=n0: e.matmul(
                        out=psO[:, :], lhsT=Sbf[:, :], rhs=qe[:, n0:n0 + 64],
                        start=False, stop=True)))
                    s.op("vector", bind(lambda e, n0=n0: e.tensor_copy(
                        out=oseg[:, n0:n0 + 64], in_=psO[:, :])))
                    s.op("tensor", bind(lambda e, c0=c0, pb=pb: e.matmul(
                        out=psS[:, :], lhsT=ketm[pb:pb + 64, c0:c0 + 128],
                        rhs=vtseg[pb:pb + 64, c0:c0 + 128], start=True, stop=True)))
                    s.op("scalar", bind(lambda e, n0=n0: e.activation(
                        out=ebl[:, :], in_=bfm[:, n0 + 63:n0 + 64], func=AF.Exp)))
                    s.op("vector", lambda e: e.tensor_tensor(out=S[:, :], in0=S[:, :],
                                                             in1=psS[:, :], op=ALU.add))
                    s.op("vector", lambda e: e.tensor_scalar(out=S[:, :], in0=S[:, :],
                                                             scalar1=ebl[:, 0:1], scalar2=None,
                                                             op0=ALU.mult))
            s.dma("gpsimd", bind(lambda e, t0=t0: e.dma_start(out=o_o[:, t0:t0 + SEG],
                                                              in_=oseg[:, :])))
        s.build(block, gsem, dsem)
    return nc


def prep_A2(r1):
    qfm = np.concatenate([r["o_qfm"] for r in r1], axis=1)
    kfm = np.concatenate([r["o_kfm"] for r in r1], axis=1)
    ktm = np.concatenate([r["o_ktm"] for r in r1], axis=0)
    vtm = np.concatenate([r["o_vtm"] for r in r1], axis=0)
    glfm = np.concatenate([r["o_glfm"] for r in r1], axis=1)
    gltm = np.concatenate([r["o_gltm"] for r in r1], axis=0)
    l2t = np.triu(np.ones((64, 64), np.float32))
    l2t = np.kron(np.eye(2, dtype=np.float32), l2t)
    maskt = np.vstack([np.triu(np.ones((64, 64), np.float32))] * 2)
    rmask = np.ones(SEG, np.float32)
    rmask[0::64] = 0.0
    rmask = np.broadcast_to(rmask, (128, SEG)).copy()
    in_maps = []
    for c in range(NC):
        h, e2 = c // 2, c % 2
        r = slice(h * 128, (h + 1) * 128)
        rv = slice(h * 256 + e2 * 128, h * 256 + e2 * 128 + 128)
        in_maps.append({
            "qfm": np.ascontiguousarray(qfm[r]), "kfm": np.ascontiguousarray(kfm[r]),
            "ktm": np.ascontiguousarray(ktm[:, r]), "vtm": np.ascontiguousarray(vtm[:, rv]),
            "glfm": np.ascontiguousarray(glfm[r]), "gltm": np.ascontiguousarray(gltm[:, r]),
            "l2t": l2t, "maskt": maskt, "rmask": rmask,
        })
    return in_maps


def run_A2(r1):
    nc = build_A2()
    res = run_bass_kernel_spmd(nc, prep_A2(r1), core_ids=CORE_IDS)
    return res.results



def build_A3():
    nc = bass.Bass("TRN2", target_bir_lowering=False, debug=False)
    x = nc.dram_tensor("x", [TL, D], F32, kind="ExternalInput").ap()
    ln = nc.dram_tensor("ln", [TL, D], F32, kind="ExternalInput").ap()
    ofm = nc.dram_tensor("ofm", [1024, TL], F32, kind="ExternalInput").ap()
    gout = nc.dram_tensor("gout", [1024, TL], F32, kind="ExternalInput").ap()
    gnw = nc.dram_tensor("gnw", [128, 2], F32, kind="ExternalInput").ap()
    wo = nc.dram_tensor("wo", [1024, D], F32, kind="ExternalInput").ap()
    mnw = nc.dram_tensor("mnw", [128, 8], F32, kind="ExternalInput").ap()
    wgate = nc.dram_tensor("wgate", [D, 2816], F32, kind="ExternalInput").ap()
    wup = nc.dram_tensor("wup", [D, 2816], F32, kind="ExternalInput").ap()
    wdown = nc.dram_tensor("wdown", [2816, D], F32, kind="ExternalInput").ap()
    o_x1 = nc.dram_tensor("o_x1", [TL, D], F32, kind="ExternalOutput").ap()

    NI = 2816 // 128
    s = Seq(nc)
    with ExitStack() as st:
        T_ = st.enter_context
        class CV:
            def __init__(self, t, base):
                self.t, self.base = t, base

            def __getitem__(self, key):
                p, f = key
                return self.t[p, self.base + f.start: self.base + f.stop]

        arena = T_(nc.sbuf_tensor("arena", [128, 39424], BF16))
        attnb = CV(arena, 0)
        wob = CV(arena, 16384)
        wgb2 = CV(arena, 0)
        wub2 = CV(arena, 11264)
        wdb2 = CV(arena, 22528)
        mball = CV(arena, 33792)
        x2t = T_(nc.sbuf_tensor("x2t", [128, NT * D], F32))
        nfm = T_(nc.sbuf_tensor("nfm", [128, 8 * TL], BF16))
        ntm = T_(nc.sbuf_tensor("ntm", [128, D], BF16))
        scr = T_(nc.sbuf_tensor("scr", [128, 3072], F32))
        oc0 = CV(scr, 0)
        oc1 = CV(scr, 512)
        gc = CV(scr, 1024)
        sqt = CV(scr, 1536)
        sg = CV(scr, 2048)
        t1 = CV(scr, 2560)
        sqd = CV(scr, 0)
        rows = T_(nc.sbuf_tensor("rows", [1, 1024], F32))
        rowa = CV(rows, 0)
        rowb = CV(rows, 512)
        gnwt = T_(nc.sbuf_tensor("gnwt", [128, 2], F32))
        mnwt = T_(nc.sbuf_tensor("mnwt", [128, 8], F32))
        epsc = T_(nc.sbuf_tensor("epsc", [128, 2], F32))
        ones128 = T_(nc.sbuf_tensor("ones128", [128, 1], F32))
        ones1f = T_(nc.sbuf_tensor("ones1f", [1, 128], F32))
        wload = T_(nc.sbuf_tensor("wload", [128, 1408], F32))
        xld = T_(nc.sbuf_tensor("xld", [128, D], F32))
        cols = T_(nc.sbuf_tensor("cols", [128, 4], F32))
        psG = T_(nc.psum_tensor("psG", [128, 512], F32))
        psU = T_(nc.psum_tensor("psU", [128, 512], F32))
        psR = T_(nc.psum_tensor("psR", [1, 512], F32))
        psB = T_(nc.psum_tensor("psB", [128, 512], F32))
        gsem = T_(nc.semaphore("gsem"))
        dsem = T_(nc.semaphore("dsem"))
        block = T_(nc.Block())

        s.dma("sync", lambda e: e.dma_start(out=gnwt[:, :], in_=gnw))
        s.dma("sync", lambda e: e.dma_start(out=mnwt[:, :], in_=mnw))
        s.op("vector", lambda e: e.memset(epsc[:, 0:1], 1e-5))
        s.op("vector", lambda e: e.memset(epsc[:, 1:2], 1e-6))
        s.op("vector", lambda e: e.memset(ones128[:, :], 1.0))
        s.op("vector", lambda e: e.memset(ones1f[:, :], 1.0))

        for h in range(4):
            for tt in range(N4):
                c0, c1 = 2 * h, 2 * h + 1
                s.dma("sync", bind(lambda e, c0=c0, tt=tt: e.dma_start(
                    out=oc0[:, 0:512], in_=ofm[c0 * 128:(c0 + 1) * 128, tt * 512:(tt + 1) * 512])))
                s.dma("sync", bind(lambda e, c1=c1, tt=tt: e.dma_start(
                    out=oc1[:, 0:512], in_=ofm[c1 * 128:(c1 + 1) * 128, tt * 512:(tt + 1) * 512])))
                s.op("scalar", lambda e: e.activation(out=sqt[:, 0:512], in_=oc0[:, 0:512], func=AF.Square))
                s.op("tensor", lambda e: e.matmul(out=psR[:, :], lhsT=ones128[:, :], rhs=sqt[:, 0:512],
                                                  start=True, stop=False))
                s.op("scalar", lambda e: e.activation(out=sqt[:, 0:512], in_=oc1[:, 0:512], func=AF.Square))
                s.op("tensor", lambda e: e.matmul(out=psR[:, :], lhsT=ones128[:, :], rhs=sqt[:, 0:512],
                                                  start=False, stop=True))
                s.op("vector", lambda e: e.tensor_scalar(out=rowa[0:1, 0:512], in0=psR[:, :],
                                                         scalar1=1.0 / 256, scalar2=None,
                                                         op0=ALU.mult))
                s.op("scalar", lambda e: e.activation(out=rowb[0:1, 0:512], in_=rowa[0:1, 0:512], func=AF.Sqrt,
                                                      bias=epsc[0:1, 1:2]))
                s.op("vector", lambda e: e.reciprocal(out=rowa[0:1, 0:512], in_=rowb[0:1, 0:512]))
                s.op("tensor", lambda e: e.matmul(out=psB[:, :], lhsT=ones1f[:, :], rhs=rowa[0:1, 0:512],
                                                  start=True, stop=True))
                for p, occ, osrc in ((0, c0, oc0), (1, c1, oc1)):
                    s.dma("sync", bind(lambda e, occ=occ, tt=tt: e.dma_start(
                        out=gc[:, 0:512], in_=gout[occ * 128:(occ + 1) * 128, tt * 512:(tt + 1) * 512])))
                    s.op("scalar", lambda e: e.activation(out=sg[:, 0:512], in_=gc[:, 0:512], func=AF.Silu))
                    s.op("vector", bind(lambda e, p=p, osrc=osrc: e.scalar_tensor_tensor(
                        out=t1[:, 0:512], in0=osrc[:, 0:512], scalar=gnwt[:, p:p + 1], in1=psB[:, :],
                        op0=ALU.mult, op1=ALU.mult)))
                    s.op("vector", bind(lambda e, occ=occ, tt=tt: e.tensor_tensor(
                        out=attnb[:, occ * TL + tt * 512: occ * TL + (tt + 1) * 512],
                        in0=t1[:, 0:512], in1=sg[:, 0:512], op=ALU.mult)))

        for c in range(8):
            s.dma("sync", bind(lambda e, c=c: e.dma_start(out=wload[:, 0:D],
                                                          in_=wo[c * 128:(c + 1) * 128, :])))
            s.op("vector", bind(lambda e, c=c: e.tensor_copy(out=wob[:, c * D:(c + 1) * D],
                                                             in_=wload[:, 0:D])))

        for i in range(NT):
            for dh in range(2):
                for c in range(8):
                    s.op("tensor", bind(lambda e, i=i, dh=dh, c=c: e.matmul(
                        out=psG[:, :],
                        lhsT=attnb[:, c * TL + i * 128: c * TL + (i + 1) * 128],
                        rhs=wob[:, c * D + dh * 512: c * D + (dh + 1) * 512],
                        start=(c == 0), stop=(c == 7))))
                s.dma("sync", bind(lambda e, i=i, dh=dh: e.dma_start(
                    out=xld[:, 0:512], in_=ln[i * 128:(i + 1) * 128, dh * 512:(dh + 1) * 512])))
                s.op("vector", bind(lambda e, i=i, dh=dh: e.tensor_tensor(
                    out=x2t[:, i * D + dh * 512: i * D + (dh + 1) * 512],
                    in0=psG[:, :], in1=xld[:, 0:512], op=ALU.add)))
            s.op("scalar", bind(lambda e, i=i: e.activation(
                out=sqd[:, 0:1024], in_=x2t[:, i * D:(i + 1) * D], func=AF.Square,
                accum_out=cols[:, 0:1])))
            s.op("vector", lambda e: e.tensor_scalar(out=cols[:, 1:2], in0=cols[:, 0:1],
                                                     scalar1=1.0 / D, scalar2=None, op0=ALU.mult))
            s.op("scalar", lambda e: e.activation(out=cols[:, 2:3], in_=cols[:, 1:2], func=AF.Sqrt,
                                                  bias=epsc[:, 1:2]))
            s.op("vector", lambda e: e.reciprocal(out=cols[:, 3:4], in_=cols[:, 2:3]))
            s.op("vector", bind(lambda e, i=i: e.tensor_scalar(
                out=ntm[:, 0:D], in0=x2t[:, i * D:(i + 1) * D],
                scalar1=cols[:, 3:4], scalar2=None, op0=ALU.mult)))
            for j in range(8):
                s.dma("sync", bind(lambda e, i=i, j=j: e.dma_start_transpose(
                    out=nfm[:, j * TL + i * 128: j * TL + (i + 1) * 128],
                    in_=ntm[:, j * 128:(j + 1) * 128])))

        for i in range(NT):
            s.dma("sync", bind(lambda e, i=i: e.dma_start(out=xld[:, :],
                                                          in_=x[i * 128:(i + 1) * 128, :])))
            s.op("vector", bind(lambda e, i=i: e.tensor_tensor(
                out=x2t[:, i * D:(i + 1) * D], in0=x2t[:, i * D:(i + 1) * D],
                in1=xld[:, :], op=ALU.add)))

        for half in range(2):
            i0 = half * 1408
            nio = 1408 // 128
            for c in range(8):
                s.dma("sync", bind(lambda e, c=c, i0=i0: e.dma_start(
                    out=wload[:, 0:1408], in_=wgate[c * 128:(c + 1) * 128, i0:i0 + 1408])))
                s.op("vector", bind(lambda e, c=c: e.tensor_scalar(
                    out=wgb2[:, c * 1408:(c + 1) * 1408], in0=wload[:, 0:1408],
                    scalar1=mnwt[:, c:c + 1], scalar2=None, op0=ALU.mult)))
                s.dma("sync", bind(lambda e, c=c, i0=i0: e.dma_start(
                    out=wload[:, 0:1408], in_=wup[c * 128:(c + 1) * 128, i0:i0 + 1408])))
                s.op("vector", bind(lambda e, c=c: e.tensor_scalar(
                    out=wub2[:, c * 1408:(c + 1) * 1408], in0=wload[:, 0:1408],
                    scalar1=mnwt[:, c:c + 1], scalar2=None, op0=ALU.mult)))
            for ic in range(nio):
                s.dma("sync", bind(lambda e, ic=ic, i0=i0: e.dma_start(
                    out=wload[:, 0:D], in_=wdown[i0 + ic * 128: i0 + (ic + 1) * 128, :])))
                s.op("vector", bind(lambda e, ic=ic: e.tensor_copy(
                    out=wdb2[:, ic * D:(ic + 1) * D], in_=wload[:, 0:D])))
            for tt in range(N4):
                for ic in range(nio):
                    for c in range(8):
                        s.op("tensor", bind(lambda e, tt=tt, ic=ic, c=c: e.matmul(
                            out=psG[:, :],
                            lhsT=wgb2[:, c * 1408 + ic * 128: c * 1408 + (ic + 1) * 128],
                            rhs=nfm[:, c * TL + tt * 512: c * TL + (tt + 1) * 512],
                            start=(c == 0), stop=(c == 7))))
                    for c in range(8):
                        s.op("tensor", bind(lambda e, tt=tt, ic=ic, c=c: e.matmul(
                            out=psU[:, :],
                            lhsT=wub2[:, c * 1408 + ic * 128: c * 1408 + (ic + 1) * 128],
                            rhs=nfm[:, c * TL + tt * 512: c * TL + (tt + 1) * 512],
                            start=(c == 0), stop=(c == 7))))
                    s.op("scalar", lambda e: e.activation(out=sg[:, 0:512], in_=psG[:, :], func=AF.Silu))
                    s.op("vector", bind(lambda e, ic=ic: e.tensor_tensor(
                        out=mball[:, ic * 512:(ic + 1) * 512], in0=sg[:, 0:512], in1=psU[:, :],
                        op=ALU.mult)))
                for q in range(4):
                    i = tt * 4 + q
                    for dh in range(2):
                        for ic in range(nio):
                            s.op("tensor", bind(lambda e, q=q, ic=ic, dh=dh: e.matmul(
                                out=psB[:, :],
                                lhsT=mball[:, ic * 512 + q * 128: ic * 512 + (q + 1) * 128],
                                rhs=wdb2[:, ic * D + dh * 512: ic * D + (dh + 1) * 512],
                                start=(ic == 0), stop=(ic == nio - 1))))
                        s.op("vector", bind(lambda e, i=i, dh=dh: e.tensor_tensor(
                            out=x2t[:, i * D + dh * 512: i * D + (dh + 1) * 512],
                            in0=x2t[:, i * D + dh * 512: i * D + (dh + 1) * 512],
                            in1=psB[:, :], op=ALU.add)))

        for i in range(NT):
            s.dma("gpsimd", bind(lambda e, i=i: e.dma_start(
                out=o_x1[i * 128:(i + 1) * 128, :], in_=x2t[:, i * D:(i + 1) * D])))
        s.build(block, gsem, dsem)
    return nc


def prep_A3(inp, r1, r2):
    ofm_full = np.concatenate([r["o_o"] for r in r2], axis=0)
    x = np.asarray(inp["L0_attn_hidden_states"], np.float32).reshape(T, D)
    common = {
        "gnw": host_cols(inp["L0_attn_gnorm_w"]),
        "wo": np.asarray(inp["L0_attn_Wo"], np.float32),
        "mnw": host_cols(inp["L0_attn_mlp_norm_w"]),
        "wgate": np.asarray(inp["L0_attn_W_gate"], np.float32),
        "wup": np.asarray(inp["L0_attn_W_up"], np.float32),
        "wdown": np.asarray(inp["L0_attn_W_down"], np.float32),
    }
    in_maps = []
    for c in range(NC):
        sl = slice(c * TL, (c + 1) * TL)
        in_maps.append(dict(common,
                            x=np.ascontiguousarray(x[sl]),
                            ln=np.ascontiguousarray(r1[c]["o_ln"]),
                            ofm=np.ascontiguousarray(ofm_full[:, sl]),
                            gout=np.ascontiguousarray(r1[c]["o_gout"])))
    return in_maps


def run_A3(inp, r1, r2):
    nc = build_A3()
    res = run_bass_kernel_spmd(nc, prep_A3(inp, r1, r2), core_ids=CORE_IDS)
    return res.results



def build_B():
    nc = bass.Bass("TRN2", target_bir_lowering=False, debug=False)
    x1 = nc.dram_tensor("x1", [TL, D], F32, kind="ExternalInput").ap()
    wfT = nc.dram_tensor("wfT", [D, D], F32, kind="ExternalInput").ap()
    wiT = nc.dram_tensor("wiT", [D, D], F32, kind="ExternalInput").ap()
    wcT = nc.dram_tensor("wcT", [D, D], F32, kind="ExternalInput").ap()
    woT = nc.dram_tensor("woT", [D, D], F32, kind="ExternalInput").ap()
    uoT = nc.dram_tensor("uoT", [D, D], F32, kind="ExternalInput").ap()
    ufT = nc.dram_tensor("ufT", [D, D], F32, kind="ExternalInput").ap()
    uiT = nc.dram_tensor("uiT", [D, D], F32, kind="ExternalInput").ap()
    ucT = nc.dram_tensor("ucT", [D, D], F32, kind="ExternalInput").ap()
    incc = nc.dram_tensor("incc", [128, 8], F32, kind="ExternalInput").ap()
    bfc = nc.dram_tensor("bfc", [128, 8], F32, kind="ExternalInput").ap()
    bic = nc.dram_tensor("bic", [128, 8], F32, kind="ExternalInput").ap()
    bcc = nc.dram_tensor("bcc", [128, 8], F32, kind="ExternalInput").ap()
    bor = nc.dram_tensor("bor", [1, D], F32, kind="ExternalInput").ap()
    o_x2m = nc.dram_tensor("o_x2m", [TL, D], F32, kind="ExternalOutput").ap()

    s = Seq(nc)
    with ExitStack() as st:
        T_ = st.enter_context

        class CV:
            def __init__(self, t, base):
                self.t, self.base = t, base

            def __getitem__(self, key):
                p, f = key
                return self.t[p, self.base + f.start: self.base + f.stop]

        x1t = T_(nc.sbuf_tensor("x1t", [128, NT * D], F32))
        ln1fm = T_(nc.sbuf_tensor("ln1fm", [128, 8 * TL], BF16))
        cfb = T_(nc.sbuf_tensor("cfb", [128, 8 * TL], BF16))
        arena = T_(nc.sbuf_tensor("arena", [128, 24576], BF16))
        lnt = T_(nc.sbuf_tensor("lnt", [128, D], F32))
        sq = T_(nc.sbuf_tensor("sq", [128, D], F32))
        ntm = T_(nc.sbuf_tensor("ntm", [128, D], BF16))
        wload = T_(nc.sbuf_tensor("wload", [128, D], F32))
        sf = T_(nc.sbuf_tensor("sf", [128, 512], F32))
        si = T_(nc.sbuf_tensor("si", [128, 512], F32))
        sc2 = T_(nc.sbuf_tensor("sc2", [128, 512], F32))
        cols = T_(nc.sbuf_tensor("cols", [128, 8], F32))
        ucols = T_(nc.sbuf_tensor("ucols", [128, 3 * 8], F32))
        inct = T_(nc.sbuf_tensor("inct", [128, 8], F32))
        bft = T_(nc.sbuf_tensor("bft", [128, 8], F32))
        bit_ = T_(nc.sbuf_tensor("bit_", [128, 8], F32))
        bct = T_(nc.sbuf_tensor("bct", [128, 8], F32))
        bort = T_(nc.sbuf_tensor("bort", [1, D], F32))
        borb = T_(nc.sbuf_tensor("borb", [1, D], BF16))
        ones1 = T_(nc.sbuf_tensor("ones1", [1, 128], BF16))
        epsc = T_(nc.sbuf_tensor("epsc", [128, 2], F32))
        psF = T_(nc.psum_tensor("psF", [128, 512], F32))
        psI = T_(nc.psum_tensor("psI", [128, 512], F32))
        psC = T_(nc.psum_tensor("psC", [128, 512], F32))
        psV = T_(nc.psum_tensor("psV", [128, 1], F32))
        gsem = T_(nc.semaphore("gsem"))
        dsem = T_(nc.semaphore("dsem"))
        block = T_(nc.Block())

        s.dma("sync", lambda e: e.dma_start(out=inct[:, :], in_=incc))
        s.dma("sync", lambda e: e.dma_start(out=bft[:, :], in_=bfc))
        s.dma("sync", lambda e: e.dma_start(out=bit_[:, :], in_=bic))
        s.dma("sync", lambda e: e.dma_start(out=bct[:, :], in_=bcc))
        s.dma("sync", lambda e: e.dma_start(out=bort[:, :], in_=bor))
        s.op("vector", lambda e: e.tensor_copy(out=borb[:, :], in_=bort[:, :]))
        s.op("vector", lambda e: e.memset(ones1[:, :], 1.0))
        s.op("vector", lambda e: e.memset(epsc[:, 0:1], 1e-5))

        for i in range(NT):
            s.dma("sync", bind(lambda e, i=i: e.dma_start(out=x1t[:, i * D:(i + 1) * D],
                                                          in_=x1[i * 128:(i + 1) * 128, :])))
            s.op("scalar", bind(lambda e, i=i: e.activation(out=sq[:, :], in_=x1t[:, i * D:(i + 1) * D],
                                                            func=AF.Square, accum_out=cols[:, 0:1])))
            s.op("vector", bind(lambda e, i=i: e.reduce_sum(out=cols[:, 1:2],
                                                            in_=x1t[:, i * D:(i + 1) * D], axis=AX.X)))
            s.op("vector", lambda e: e.tensor_scalar(out=cols[:, 2:3], in0=cols[:, 1:2],
                                                     scalar1=1.0 / D, scalar2=None, op0=ALU.mult))
            s.op("vector", lambda e: e.tensor_scalar(out=cols[:, 3:4], in0=cols[:, 0:1],
                                                     scalar1=1.0 / D, scalar2=None, op0=ALU.mult))
            s.op("vector", lambda e: e.scalar_tensor_tensor(out=cols[:, 4:5], in0=cols[:, 2:3],
                                                            scalar=cols[:, 2:3], in1=cols[:, 3:4],
                                                            op0=ALU.mult, op1=ALU.subtract))
            s.op("scalar", lambda e: e.activation(out=cols[:, 5:6], in_=cols[:, 4:5], func=AF.Sqrt,
                                                  scale=-1.0, bias=epsc[:, 0:1]))
            s.op("vector", lambda e: e.reciprocal(out=cols[:, 6:7], in_=cols[:, 5:6]))
            s.op("vector", bind(lambda e, i=i: e.tensor_scalar(out=ntm[:, :],
                                                               in0=x1t[:, i * D:(i + 1) * D],
                                                               scalar1=cols[:, 2:3], scalar2=cols[:, 6:7],
                                                               op0=ALU.subtract, op1=ALU.mult)))
            for j in range(8):
                s.dma("sync", bind(lambda e, i=i, j=j: e.dma_start_transpose(
                    out=ln1fm[:, j * TL + i * 128: j * TL + (i + 1) * 128],
                    in_=ntm[:, j * 128:(j + 1) * 128])))

        for g, (uT, bcol) in enumerate(((ufT, bft), (uiT, bit_), (ucT, bct))):
            for mc in range(8):
                for kc in range(8):
                    s.dma("sync", bind(lambda e, uT=uT, kc=kc, mc=mc: e.dma_start(
                        out=wload[:, 0:128], in_=uT[kc * 128:(kc + 1) * 128, mc * 128:(mc + 1) * 128])))
                    s.op("tensor", bind(lambda e, kc=kc: e.matmul(
                        out=psV[:, :], lhsT=wload[:, 0:128], rhs=inct[:, kc:kc + 1],
                        start=(kc == 0), stop=(kc == 7))))
                s.op("vector", bind(lambda e, g=g, mc=mc, bcol=bcol: e.tensor_tensor(
                    out=ucols[:, g * 8 + mc: g * 8 + mc + 1], in0=psV[:, :],
                    in1=bcol[:, mc:mc + 1], op=ALU.add)))

        wfb, wib, wcb = CV(arena, 0), CV(arena, 8192), CV(arena, 16384)
        for (w_ap, dst) in ((wfT, wfb), (wiT, wib), (wcT, wcb)):
            for c in range(8):
                s.dma("sync", bind(lambda e, w_ap=w_ap, c=c: e.dma_start(
                    out=wload[:, 0:D], in_=w_ap[c * 128:(c + 1) * 128, :])))
                s.op("vector", bind(lambda e, dst=dst, c=c: e.tensor_copy(
                    out=dst[:, c * D:(c + 1) * D], in_=wload[:, 0:D])))

        for mc in range(8):
            for tt in range(N4):
                for (ps, wb) in ((psF, wfb), (psI, wib), (psC, wcb)):
                    for c in range(8):
                        s.op("tensor", bind(lambda e, ps=ps, wb=wb, mc=mc, tt=tt, c=c: e.matmul(
                            out=ps[:, :],
                            lhsT=wb[:, c * D + mc * 128: c * D + (mc + 1) * 128],
                            rhs=ln1fm[:, c * TL + tt * 512: c * TL + (tt + 1) * 512],
                            start=(c == 0), stop=(c == 7))))
                s.op("scalar", bind(lambda e, mc=mc: e.activation(
                    out=sf[:, :], in_=psF[:, :], func=AF.Sigmoid, bias=ucols[:, mc:mc + 1])))
                s.op("scalar", bind(lambda e, mc=mc: e.activation(
                    out=si[:, :], in_=psI[:, :], func=AF.Sigmoid, bias=ucols[:, 8 + mc:8 + mc + 1])))
                s.op("scalar", bind(lambda e, mc=mc: e.activation(
                    out=sc2[:, :], in_=psC[:, :], func=AF.Tanh, bias=ucols[:, 16 + mc:16 + mc + 1])))
                s.op("vector", lambda e: e.tensor_tensor(out=si[:, :], in0=si[:, :], in1=sc2[:, :],
                                                         op=ALU.mult))
                s.op("vector", bind(lambda e, mc=mc, tt=tt: e.scalar_tensor_tensor(
                    out=cfb[:, mc * TL + tt * 512: mc * TL + (tt + 1) * 512],
                    in0=sf[:, :], scalar=inct[:, mc:mc + 1], in1=si[:, :],
                    op0=ALU.mult, op1=ALU.add)))

        wob2, uob2 = CV(arena, 0), CV(arena, 8192)
        for (w_ap, dst) in ((woT, wob2), (uoT, uob2)):
            for c in range(8):
                s.dma("sync", bind(lambda e, w_ap=w_ap, c=c: e.dma_start(
                    out=wload[:, 0:D], in_=w_ap[c * 128:(c + 1) * 128, :])))
                s.op("vector", bind(lambda e, dst=dst, c=c: e.tensor_copy(
                    out=dst[:, c * D:(c + 1) * D], in_=wload[:, 0:D])))

        for i in range(NT):
            for oh in range(2):
                for c in range(8):
                    s.op("tensor", bind(lambda e, i=i, oh=oh, c=c: e.matmul(
                        out=psF[:, :],
                        lhsT=ln1fm[:, c * TL + i * 128: c * TL + (i + 1) * 128],
                        rhs=wob2[:, c * D + oh * 512: c * D + (oh + 1) * 512],
                        start=(c == 0), stop=False)))
                for c in range(8):
                    s.op("tensor", bind(lambda e, i=i, oh=oh, c=c: e.matmul(
                        out=psF[:, :],
                        lhsT=cfb[:, c * TL + i * 128: c * TL + (i + 1) * 128],
                        rhs=uob2[:, c * D + oh * 512: c * D + (oh + 1) * 512],
                        start=False, stop=False)))
                s.op("tensor", bind(lambda e, oh=oh: e.matmul(
                    out=psF[:, :], lhsT=ones1[0:1, :], rhs=borb[0:1, oh * 512:(oh + 1) * 512],
                    start=False, stop=True)))
                s.op("scalar", lambda e: e.activation(out=sf[:, :], in_=psF[:, :], func=AF.Tanh))
                s.op("vector", bind(lambda e, i=i, oh=oh: e.tensor_tensor(
                    out=x1t[:, i * D + oh * 512: i * D + (oh + 1) * 512],
                    in0=x1t[:, i * D + oh * 512: i * D + (oh + 1) * 512],
                    in1=sf[:, :], op=ALU.add)))
            s.dma("gpsimd", bind(lambda e, i=i: e.dma_start(
                out=o_x2m[i * 128:(i + 1) * 128, :], in_=x1t[:, i * D:(i + 1) * D])))
        s.build(block, gsem, dsem)
    return nc


def prep_B(inp, r3):
    tr = lambda k: np.ascontiguousarray(np.asarray(inp[k], np.float32).T)
    common = {
        "wfT": tr("L0_ffn_wf"), "wiT": tr("L0_ffn_wi"), "wcT": tr("L0_ffn_wc"),
        "woT": tr("L0_ffn_wo"), "uoT": tr("L0_ffn_uo"),
        "ufT": tr("L0_ffn_uf"), "uiT": tr("L0_ffn_ui"), "ucT": tr("L0_ffn_uc"),
        "incc": host_cols(inp["L0_ffn_in_c"]),
        "bfc": host_cols(inp["L0_ffn_bf"]), "bic": host_cols(inp["L0_ffn_bi"]),
        "bcc": host_cols(inp["L0_ffn_bc"]), "bor": host_row(inp["L0_ffn_bo"]),
    }
    return [dict(common, x1=np.ascontiguousarray(r3[c]["o_x1"])) for c in range(NC)]


def run_B(inp, r3):
    nc = build_B()
    res = run_bass_kernel_spmd(nc, prep_B(inp, r3), core_ids=CORE_IDS)
    return res.results



def build_C1():
    nc = bass.Bass("TRN2", target_bir_lowering=False, debug=False)
    x2m = nc.dram_tensor("x2m", [TL, D], F32, kind="ExternalInput").ap()
    wh = nc.dram_tensor("wh", [D, 4096], F32, kind="ExternalInput").ap()
    wqk = nc.dram_tensor("wqk", [D, 128], F32, kind="ExternalInput").ap()
    lng = nc.dram_tensor("lng", [128, 8], F32, kind="ExternalInput").ap()
    lnb = nc.dram_tensor("lnb", [128, 8], F32, kind="ExternalInput").ap()
    bhg = nc.dram_tensor("bhg", [128, 16], F32, kind="ExternalInput").ap()
    bhvr = nc.dram_tensor("bhvr", [1, 2048], BF16, kind="ExternalInput").ap()
    bqkc = nc.dram_tensor("bqkc", [128, 1], F32, kind="ExternalInput").ap()
    bqkr = nc.dram_tensor("bqkr", [1, 128], BF16, kind="ExternalInput").ap()
    gmc = nc.dram_tensor("gmc", [128, 4], F32, kind="ExternalInput").ap()
    btc = nc.dram_tensor("btc", [128, 4], F32, kind="ExternalInput").ap()
    gmr3 = nc.dram_tensor("gmr3", [1, 128], BF16, kind="ExternalInput").ap()
    btr3 = nc.dram_tensor("btr3", [1, 128], BF16, kind="ExternalInput").ap()
    sinf = nc.dram_tensor("sinf", [64, TL], BF16, kind="ExternalInput").ap()
    cosf = nc.dram_tensor("cosf", [64, TL], BF16, kind="ExternalInput").ap()
    sintm = nc.dram_tensor("sintm", [128, NT * 64], BF16, kind="ExternalInput").ap()
    costm = nc.dram_tensor("costm", [128, NT * 64], BF16, kind="ExternalInput").ap()

    o_lnx = nc.dram_tensor("o_lnx", [TL, D], F32, kind="ExternalOutput").ap()
    o_gate = nc.dram_tensor("o_gate", [2048, TL], BF16, kind="ExternalOutput").ap()
    o_quad = nc.dram_tensor("o_quad", [2048, TL], BF16, kind="ExternalOutput").ap()
    o_linq = nc.dram_tensor("o_linq", [128, TL], BF16, kind="ExternalOutput").ap()
    o_linkv = nc.dram_tensor("o_linkv", [128, 2048], F32, kind="ExternalOutput").ap()

    s = Seq(nc)
    with ExitStack() as st:
        T_ = st.enter_context

        class CV:
            def __init__(self, t, base):
                self.t, self.base = t, base

            def __getitem__(self, key):
                p, f = key
                return self.t[p, self.base + f.start: self.base + f.stop]

        vtm = T_(nc.sbuf_tensor("vtm", [128, NT * 2048], BF16))
        nfm = T_(nc.sbuf_tensor("nfm", [128, 8 * TL], BF16))
        arena = T_(nc.sbuf_tensor("arena", [128, 16384], BF16))
        u1 = T_(nc.sbuf_tensor("u1", [128, 2048], F32))
        xt = CV(u1, 0)
        lnt = CV(u1, 1024)
        vrlo = u1
        sq = T_(nc.sbuf_tensor("sq", [128, D], F32))
        scr = CV(sq, 0)
        ntm2 = T_(nc.sbuf_tensor("ntm2", [128, D], BF16))
        cols = T_(nc.sbuf_tensor("cols", [128, 8], F32))
        epsc = T_(nc.sbuf_tensor("epsc", [128, 2], F32))
        lngt = T_(nc.sbuf_tensor("lngt", [128, 8], F32))
        lnbt = T_(nc.sbuf_tensor("lnbt", [128, 8], F32))
        bhgt = T_(nc.sbuf_tensor("bhgt", [128, 16], F32))
        biasvr = T_(nc.sbuf_tensor("biasvr", [1, 2048], BF16))
        bqkct = T_(nc.sbuf_tensor("bqkct", [128, 1], F32))
        biasqr = T_(nc.sbuf_tensor("biasqr", [1, 128], BF16))
        gmct = T_(nc.sbuf_tensor("gmct", [128, 4], F32))
        btct = T_(nc.sbuf_tensor("btct", [128, 4], F32))
        gmr3t = T_(nc.sbuf_tensor("gmr3t", [1, 128], BF16))
        btr3t = T_(nc.sbuf_tensor("btr3t", [1, 128], BF16))
        g3bc = T_(nc.sbuf_tensor("g3bc", [128, 128], F32))
        b3bc = T_(nc.sbuf_tensor("b3bc", [128, 128], F32))
        scf = T_(nc.sbuf_tensor("scf", [64, 2 * TL], BF16))
        sc64 = T_(nc.sbuf_tensor("sc64", [64, 2 * TL], BF16))
        sintmt = T_(nc.sbuf_tensor("sintmt", [128, NT * 64], BF16))
        costmt = T_(nc.sbuf_tensor("costmt", [128, NT * 64], BF16))
        qkfm = T_(nc.sbuf_tensor("qkfm", [128, TL], BF16))
        qktm = T_(nc.sbuf_tensor("qktm", [128, 128], BF16))
        qq = T_(nc.sbuf_tensor("qq", [128, TL], BF16))
        qk2 = T_(nc.sbuf_tensor("qk2", [128, TL], BF16))
        lq = T_(nc.sbuf_tensor("lq", [128, TL], BF16))
        lk = T_(nc.sbuf_tensor("lk", [128, NT * 128], BF16))
        vrt = T_(nc.sbuf_tensor("vrt", [128, 256], F32))
        vrhi = T_(nc.sbuf_tensor("vrhi", [64, TL], F32))
        attnji = T_(nc.sbuf_tensor("attnji", [128, 2 * 128], BF16))
        stgb = T_(nc.sbuf_tensor("stgb", [128, 512], BF16))
        wload = T_(nc.sbuf_tensor("wload", [128, 1024], F32))
        ones1 = T_(nc.sbuf_tensor("ones1", [1, 128], BF16))
        ps0 = T_(nc.psum_tensor("ps0", [128, 512], F32))
        ps1 = T_(nc.psum_tensor("ps1", [128, 512], F32))
        kv0 = T_(nc.psum_tensor("kv0", [128, 512], F32))
        kv1 = T_(nc.psum_tensor("kv1", [128, 512], F32))
        kv2 = T_(nc.psum_tensor("kv2", [128, 512], F32))
        kv3 = T_(nc.psum_tensor("kv3", [128, 512], F32))
        gsem = T_(nc.semaphore("gsem"))
        dsem = T_(nc.semaphore("dsem"))
        block = T_(nc.Block())
        kvps = (kv0, kv1, kv2, kv3)

        for (dst, srcp) in ((lngt, lng), (lnbt, lnb), (bhgt, bhg), (biasvr, bhvr),
                            (bqkct, bqkc), (biasqr, bqkr), (gmct, gmc), (btct, btc),
                            (gmr3t, gmr3), (btr3t, btr3),
                            (sintmt, sintm), (costmt, costm)):
            s.dma("sync", bind(lambda e, dst=dst, srcp=srcp: e.dma_start(out=dst[:, :], in_=srcp)))
        s.dma("sync", lambda e: e.dma_start(out=scf[:, 0:TL], in_=sinf))
        s.dma("sync", lambda e: e.dma_start(out=scf[:, TL:2 * TL], in_=cosf))
        s.op("vector", lambda e: e.memset(epsc[:, 0:1], 1e-5))
        s.op("vector", lambda e: e.memset(ones1[:, :], 1.0))
        s.op("tensor", lambda e: e.matmul(out=ps0[:, 0:128], lhsT=ones1[:, :], rhs=gmr3t[:, :],
                                          start=True, stop=True))
        s.op("vector", lambda e: e.tensor_copy(out=g3bc[:, :], in_=ps0[:, 0:128]))
        s.op("tensor", lambda e: e.matmul(out=ps0[:, 0:128], lhsT=ones1[:, :], rhs=btr3t[:, :],
                                          start=True, stop=True))
        s.op("vector", lambda e: e.tensor_copy(out=b3bc[:, :], in_=ps0[:, 0:128]))

        def ln_pass(src_ap, dst_tile, eps_col):
            s.op("scalar", lambda e: e.activation(out=sq[:, :], in_=src_ap, func=AF.Square,
                                                  accum_out=cols[:, 0:1]))
            s.op("vector", lambda e: e.reduce_sum(out=cols[:, 1:2], in_=src_ap, axis=AX.X))
            s.op("vector", lambda e: e.tensor_scalar(out=cols[:, 2:3], in0=cols[:, 1:2],
                                                     scalar1=1.0 / D, scalar2=None, op0=ALU.mult))
            s.op("vector", lambda e: e.tensor_scalar(out=cols[:, 3:4], in0=cols[:, 0:1],
                                                     scalar1=1.0 / D, scalar2=None, op0=ALU.mult))
            s.op("vector", lambda e: e.scalar_tensor_tensor(out=cols[:, 4:5], in0=cols[:, 2:3],
                                                            scalar=cols[:, 2:3], in1=cols[:, 3:4],
                                                            op0=ALU.mult, op1=ALU.subtract))
            s.op("scalar", bind(lambda e, eps_col=eps_col: e.activation(
                out=cols[:, 5:6], in_=cols[:, 4:5], func=AF.Sqrt, scale=-1.0, bias=eps_col)))
            s.op("vector", lambda e: e.reciprocal(out=cols[:, 6:7], in_=cols[:, 5:6]))
            s.op("vector", lambda e: e.tensor_scalar(out=dst_tile, in0=src_ap,
                                                     scalar1=cols[:, 2:3], scalar2=cols[:, 6:7],
                                                     op0=ALU.subtract, op1=ALU.mult))

        for i in range(NT):
            s.dma("sync", bind(lambda e, i=i: e.dma_start(out=xt[:, 0:D],
                                                          in_=x2m[i * 128:(i + 1) * 128, :])))
            ln_pass(xt[:, 0:D], lnt[:, 0:D], epsc[:, 0:1])
            s.dma("gpsimd", bind(lambda e, i=i: e.dma_start(out=o_lnx[i * 128:(i + 1) * 128, :],
                                                            in_=lnt[:, 0:D])))
            ln_pass(lnt[:, 0:D], ntm2[:, 0:D], epsc[:, 0:1])
            for j in range(8):
                s.dma("sync", bind(lambda e, i=i, j=j: e.dma_start_transpose(
                    out=nfm[:, j * TL + i * 128: j * TL + (i + 1) * 128],
                    in_=ntm2[:, j * 128:(j + 1) * 128])))
        for j in range(8):
            s.op("vector", bind(lambda e, j=j: e.tensor_scalar(
                out=nfm[:, j * TL:(j + 1) * TL], in0=nfm[:, j * TL:(j + 1) * TL],
                scalar1=lngt[:, j:j + 1], scalar2=lnbt[:, j:j + 1],
                op0=ALU.mult, op1=ALU.add)))

        wqkb = T_(nc.sbuf_tensor("wqkb", [128, 8 * 128], BF16))
        for c in range(8):
            s.dma("sync", bind(lambda e, c=c: e.dma_start(out=wload[:, 0:128],
                                                          in_=wqk[c * 128:(c + 1) * 128, :])))
            s.op("vector", bind(lambda e, c=c: e.tensor_copy(
                out=wqkb[:, c * 128:(c + 1) * 128], in_=wload[:, 0:128])))

        for tt in range(N4):
            for c in range(8):
                s.op("tensor", bind(lambda e, tt=tt, c=c: e.matmul(
                    out=ps0[:, :], lhsT=wqkb[:, c * 128:(c + 1) * 128],
                    rhs=nfm[:, c * TL + tt * 512: c * TL + (tt + 1) * 512],
                    start=(c == 0), stop=(c == 7))))
            s.op("scalar", bind(lambda e, tt=tt: e.activation(
                out=qkfm[:, tt * 512:(tt + 1) * 512], in_=ps0[:, :], func=AF.Silu,
                bias=bqkct[:, 0:1])))

        def linK(i):
            q0 = i * 128
            s.op("vector", lambda e: e.tensor_tensor(
                out=vrt[:, 0:128], in0=qktm[:, 0:128], in1=g3bc[:, :], op=ALU.mult))
            s.op("vector", lambda e: e.tensor_tensor(out=vrt[:, 0:128], in0=vrt[:, 0:128],
                                                     in1=b3bc[:, :], op=ALU.add))
            t0 = i * 64
            s.op("vector", bind(lambda e, t0=t0: e.tensor_tensor(
                out=vrt[:, 128:192], in0=vrt[:, 0:64], in1=costmt[:, t0:t0 + 64], op=ALU.mult)))
            s.op("vector", bind(lambda e, t0=t0: e.tensor_tensor(
                out=vrt[:, 192:256], in0=vrt[:, 64:128], in1=sintmt[:, t0:t0 + 64], op=ALU.mult)))
            s.op("vector", bind(lambda e, q0=q0: e.tensor_tensor(
                out=lk[:, q0:q0 + 64], in0=vrt[:, 128:192], in1=vrt[:, 192:256], op=ALU.subtract)))
            s.op("vector", bind(lambda e, t0=t0: e.tensor_tensor(
                out=vrt[:, 128:192], in0=vrt[:, 64:128], in1=costmt[:, t0:t0 + 64], op=ALU.mult)))
            s.op("vector", bind(lambda e, t0=t0: e.tensor_tensor(
                out=vrt[:, 192:256], in0=vrt[:, 0:64], in1=sintmt[:, t0:t0 + 64], op=ALU.mult)))
            s.op("vector", bind(lambda e, q0=q0: e.tensor_tensor(
                out=lk[:, q0 + 64:q0 + 128], in0=vrt[:, 128:192], in1=vrt[:, 192:256], op=ALU.add)))

        for i in range(NT):
            for c in range(8):
                s.op("tensor", bind(lambda e, i=i, c=c: e.matmul(
                    out=ps1[:, 0:128], lhsT=nfm[:, c * TL + i * 128: c * TL + (i + 1) * 128],
                    rhs=wqkb[:, c * 128:(c + 1) * 128], start=(c == 0), stop=False)))
            s.op("tensor", lambda e: e.matmul(out=ps1[:, 0:128], lhsT=ones1[0:1, :],
                                              rhs=biasqr[0:1, :], start=False, stop=True))
            s.op("scalar", lambda e: e.activation(
                out=qktm[:, 0:128], in_=ps1[:, 0:128], func=AF.Silu))
            linK(i)

        def rope_fm(dst, gcol, bcol, glo, blo):
            s.op("vector", bind(lambda e, gcol=gcol, bcol=bcol: e.tensor_scalar(
                out=vrlo[0:64, 0:TL], in0=qkfm[0:64, 0:TL], scalar1=gcol, scalar2=bcol,
                op0=ALU.mult, op1=ALU.add)))
            s.op("vector", bind(lambda e, glo=glo, blo=blo: e.tensor_scalar(
                out=vrhi[:, 0:TL], in0=qkfm[64:128, 0:TL],
                scalar1=glo, scalar2=blo, op0=ALU.mult, op1=ALU.add)))
            s.op("vector", lambda e: e.tensor_tensor(out=sc64[:, 0:TL], in0=vrlo[0:64, 0:TL],
                                                     in1=scf[:, TL:2 * TL], op=ALU.mult))
            s.op("vector", lambda e: e.tensor_tensor(out=sc64[:, TL:2 * TL],
                                                     in0=vrhi[:, 0:TL],
                                                     in1=scf[:, 0:TL], op=ALU.mult))
            s.op("vector", bind(lambda e, dst=dst: e.tensor_tensor(
                out=dst[0:64, 0:TL], in0=sc64[:, 0:TL], in1=sc64[:, TL:2 * TL], op=ALU.subtract)))
            s.op("vector", lambda e: e.tensor_tensor(out=sc64[:, 0:TL],
                                                     in0=vrhi[:, 0:TL],
                                                     in1=scf[:, TL:2 * TL], op=ALU.mult))
            s.op("vector", lambda e: e.tensor_tensor(out=sc64[:, TL:2 * TL], in0=vrlo[0:64, 0:TL],
                                                     in1=scf[:, 0:TL], op=ALU.mult))
            s.op("vector", bind(lambda e, dst=dst: e.tensor_tensor(
                out=dst[64:128, 0:TL], in0=sc64[:, 0:TL], in1=sc64[:, TL:2 * TL], op=ALU.add)))

        rope_fm(qq, gmct[0:64, 0:1], btct[0:64, 0:1], gmct[64:128, 0:1], btct[64:128, 0:1])
        rope_fm(qk2, gmct[0:64, 2:3], btct[0:64, 2:3], gmct[64:128, 2:3], btct[64:128, 2:3])
        rope_fm(lq, gmct[0:64, 1:2], btct[0:64, 1:2], gmct[64:128, 1:2], btct[64:128, 1:2])
        s.dma("gpsimd", lambda e: e.dma_start(out=o_linq, in_=lq[:, 0:TL]))

        whb = CV(arena, 0)
        for c in range(8):
            for hh in range(2):
                s.dma("sync", bind(lambda e, c=c, hh=hh: e.dma_start(
                    out=wload[:, 0:1024],
                    in_=wh[c * 128:(c + 1) * 128, 2048 + hh * 1024: 2048 + (hh + 1) * 1024])))
                s.op("vector", bind(lambda e, c=c, hh=hh: e.tensor_copy(
                    out=whb[:, c * 2048 + hh * 1024: c * 2048 + (hh + 1) * 1024],
                    in_=wload[:, 0:1024])))
        for jc in range(16):
            for tt in range(N4):
                for c in range(8):
                    s.op("tensor", bind(lambda e, jc=jc, tt=tt, c=c: e.matmul(
                        out=ps0[:, :], lhsT=whb[:, c * 2048 + jc * 128: c * 2048 + (jc + 1) * 128],
                        rhs=nfm[:, c * TL + tt * 512: c * TL + (tt + 1) * 512],
                        start=(c == 0), stop=(c == 7))))
                s.op("scalar", bind(lambda e, jc=jc: e.activation(
                    out=stgb[:, :], in_=ps0[:, :], func=AF.Silu, bias=bhgt[:, jc:jc + 1])))
                s.dma("gpsimd", bind(lambda e, jc=jc, tt=tt: e.dma_start(
                    out=o_gate[jc * 128:(jc + 1) * 128, tt * 512:(tt + 1) * 512],
                    in_=stgb[:, :])))

        for c in range(8):
            for hh in range(2):
                s.dma("sync", bind(lambda e, c=c, hh=hh: e.dma_start(
                    out=wload[:, 0:1024],
                    in_=wh[c * 128:(c + 1) * 128, hh * 1024:(hh + 1) * 1024])))
                s.op("vector", bind(lambda e, c=c, hh=hh: e.tensor_copy(
                    out=whb[:, c * 2048 + hh * 1024: c * 2048 + (hh + 1) * 1024],
                    in_=wload[:, 0:1024])))
        for i in range(NT):
            for js in range(4):
                for c in range(8):
                    s.op("tensor", bind(lambda e, i=i, js=js, c=c: e.matmul(
                        out=ps0[:, :], lhsT=nfm[:, c * TL + i * 128: c * TL + (i + 1) * 128],
                        rhs=whb[:, c * 2048 + js * 512: c * 2048 + (js + 1) * 512],
                        start=(c == 0), stop=False)))
                s.op("tensor", bind(lambda e, js=js: e.matmul(
                    out=ps0[:, :], lhsT=ones1[0:1, :], rhs=biasvr[0:1, js * 512:(js + 1) * 512],
                    start=False, stop=True)))
                s.op("scalar", bind(lambda e, i=i, js=js: e.activation(
                    out=vtm[:, i * 2048 + js * 512: i * 2048 + (js + 1) * 512],
                    in_=ps0[:, :], func=AF.Silu)))

        for es in range(4):
            for i in range(NT):
                s.op("tensor", bind(lambda e, es=es, i=i: e.matmul(
                    out=kvps[es][:, :], lhsT=lk[:, i * 128:(i + 1) * 128],
                    rhs=vtm[:, i * 2048 + es * 512: i * 2048 + (es + 1) * 512],
                    start=(i == 0), stop=(i == NT - 1))))
            s.op("vector", bind(lambda e, es=es: e.tensor_scalar(
                out=sq[:, (es % 2) * 512:(es % 2 + 1) * 512], in0=kvps[es][:, :],
                scalar1=1.0 / T, scalar2=None, op0=ALU.mult)))
            if es % 2 == 1:
                s.dma("gpsimd", bind(lambda e, es=es: e.dma_start(
                    out=o_linkv[:, (es // 2) * 1024:(es // 2 + 1) * 1024], in_=sq[:, 0:1024])))

        for g in range(8):
            for it in range(2):
                i_t = g * 256 + it * 128
                for jt in range(2):
                    j_t = g * 256 + jt * 128
                    s.op("tensor", bind(lambda e, i_t=i_t, j_t=j_t: e.matmul(
                        out=ps0[:, 0:128], lhsT=qk2[:, j_t:j_t + 128], rhs=qq[:, i_t:i_t + 128],
                        start=True, stop=True)))
                    s.op("vector", lambda e: e.tensor_scalar(
                        out=scr[:, 0:128], in0=ps0[:, 0:128], scalar1=0.0, scalar2=1.0 / 256,
                        op0=ALU.max, op1=ALU.mult))
                    s.op("vector", bind(lambda e, jt=jt: e.tensor_tensor(
                        out=attnji[:, jt * 128:(jt + 1) * 128], in0=scr[:, 0:128],
                        in1=scr[:, 0:128], op=ALU.mult)))
                for es in range(16):
                    for jt in range(2):
                        s.op("tensor", bind(lambda e, g=g, jt=jt, es=es: e.matmul(
                            out=ps1[:, 0:128],
                            lhsT=vtm[:, (2 * g + jt) * 2048 + es * 128:
                                     (2 * g + jt) * 2048 + (es + 1) * 128],
                            rhs=attnji[:, jt * 128:(jt + 1) * 128],
                            start=(jt == 0), stop=(jt == 1))))
                    s.op("vector", lambda e: e.tensor_copy(out=stgb[:, 0:128], in_=ps1[:, 0:128]))
                    s.dma("gpsimd", bind(lambda e, es=es, i_t=i_t: e.dma_start(
                        out=o_quad[es * 128:(es + 1) * 128, i_t:i_t + 128],
                        in_=stgb[:, 0:128])))
        s.build(block, gsem, dsem)
    return nc


def rope_tables(pos):
    inv = np.power(10000.0, np.arange(64, dtype=np.float64) / 64.0)
    ang = pos[:, None].astype(np.float64) * inv[None, :]
    return np.sin(ang).astype(np.float32), np.cos(ang).astype(np.float32)


def prep_C1(inp, rb):
    wh = np.asarray(inp["L1_attn_W_hidden"], np.float32)
    bh = np.asarray(inp["L1_attn_b_hidden"], np.float32)
    gm = np.asarray(inp["L1_attn_os_gamma"], np.float32)
    bt = np.asarray(inp["L1_attn_os_beta"], np.float32)
    bfc = lambda a: np.ascontiguousarray(a).astype(ml_dtypes.bfloat16)
    common = {
        "wh": wh, "wqk": np.asarray(inp["L1_attn_W_qk"], np.float32),
        "lng": host_cols(inp["L1_attn_ln_g"]), "lnb": host_cols(inp["L1_attn_ln_b"]),
        "bhg": host_cols(bh[2048:4096]), "bhvr": bfc(host_row(bh[0:2048])),
        "bqkc": host_cols(inp["L1_attn_b_qk"]), "bqkr": bfc(host_row(inp["L1_attn_b_qk"])),
        "gmc": np.ascontiguousarray(gm.T), "btc": np.ascontiguousarray(bt.T),
        "gmr3": bfc(host_row(gm[3])), "btr3": bfc(host_row(bt[3])),
    }
    in_maps = []
    for c in range(NC):
        pos = np.arange(c * TL, (c + 1) * TL)
        sn, cs = rope_tables(pos)
        sn_tm = np.ascontiguousarray(sn.reshape(NT, 128, 64).transpose(1, 0, 2).reshape(128, NT * 64))
        cs_tm = np.ascontiguousarray(cs.reshape(NT, 128, 64).transpose(1, 0, 2).reshape(128, NT * 64))
        in_maps.append(dict(common,
                            x2m=np.ascontiguousarray(rb[c]["o_x2m"]),
                            sinf=bfc(sn.T), cosf=bfc(cs.T),
                            sintm=bfc(sn_tm), costm=bfc(cs_tm)))
    return in_maps


def run_C1(inp, rb):
    nc = build_C1()
    res = run_bass_kernel_spmd(nc, prep_C1(inp, rb), core_ids=CORE_IDS)
    return res.results



def build_C2():
    nc = bass.Bass("TRN2", target_bir_lowering=False, debug=False)
    x2m = nc.dram_tensor("x2m", [TL, D], F32, kind="ExternalInput").ap()
    lnx = nc.dram_tensor("lnx", [TL, D], F32, kind="ExternalInput").ap()
    kvp = nc.dram_tensor("kvp", [8, 128, 2048], F32, kind="ExternalInput").ap()
    linq = nc.dram_tensor("linq", [128, TL], BF16, kind="ExternalInput").ap()
    quad = nc.dram_tensor("quad", [2048, TL], BF16, kind="ExternalInput").ap()
    gate = nc.dram_tensor("gate", [2048, TL], BF16, kind="ExternalInput").ap()
    wout = nc.dram_tensor("wout", [2048, D], F32, kind="ExternalInput").ap()
    bor = nc.dram_tensor("bor", [1, D], BF16, kind="ExternalInput").ap()
    o_x3 = nc.dram_tensor("o_x3", [TL, D], F32, kind="ExternalOutput").ap()

    s = Seq(nc)
    with ExitStack() as st:
        T_ = st.enter_context
        kva = T_(nc.sbuf_tensor("kva", [128, 2048], F32))
        kvb = T_(nc.sbuf_tensor("kvb", [128, 2048], F32))
        kvbf = T_(nc.sbuf_tensor("kvbf", [128, 2048], BF16))
        lqt = T_(nc.sbuf_tensor("lqt", [128, TL], BF16))
        wob = T_(nc.sbuf_tensor("wob", [128, 16 * D], BF16))
        outat = T_(nc.sbuf_tensor("outat", [128, 16 * 512], BF16))
        qdt = T_(nc.sbuf_tensor("qdt", [128, 512], BF16))
        gtt = T_(nc.sbuf_tensor("gtt", [128, 512], BF16))
        scr = T_(nc.sbuf_tensor("scr2", [128, 512], F32))
        lnxt = T_(nc.sbuf_tensor("lnxt", [128, 512], F32))
        x2mt = T_(nc.sbuf_tensor("x2mt", [128, 512], F32))
        x3t = T_(nc.sbuf_tensor("x3t", [128, 512], F32))
        bort = T_(nc.sbuf_tensor("bort", [1, D], BF16))
        ones1 = T_(nc.sbuf_tensor("ones1c", [1, 128], BF16))
        wload = T_(nc.sbuf_tensor("wloadc", [128, D], F32))
        ps0 = T_(nc.psum_tensor("ps0c", [128, 512], F32))
        ps1 = T_(nc.psum_tensor("ps1c", [128, 512], F32))
        gsem = T_(nc.semaphore("gsem"))
        dsem = T_(nc.semaphore("dsem"))
        block = T_(nc.Block())

        s.dma("sync", lambda e: e.dma_start(out=bort[:, :], in_=bor))
        s.dma("sync", lambda e: e.dma_start(out=lqt[:, :], in_=linq))
        s.op("vector", lambda e: e.memset(ones1[:, :], 1.0))
        s.dma("sync", lambda e: e.dma_start(out=kva[:, :], in_=kvp[0]))
        for c in range(1, 8):
            s.dma("sync", bind(lambda e, c=c: e.dma_start(out=kvb[:, :], in_=kvp[c])))
            s.op("vector", lambda e: e.tensor_tensor(out=kva[:, :], in0=kva[:, :],
                                                     in1=kvb[:, :], op=ALU.add))
        s.op("vector", lambda e: e.tensor_copy(out=kvbf[:, :], in_=kva[:, :]))
        for ec in range(16):
            s.dma("sync", bind(lambda e, ec=ec: e.dma_start(out=wload[:, :],
                                                            in_=wout[ec * 128:(ec + 1) * 128, :])))
            s.op("vector", bind(lambda e, ec=ec: e.tensor_copy(out=wob[:, ec * D:(ec + 1) * D],
                                                               in_=wload[:, :])))
        for tt in range(N4):
            for ec in range(16):
                s.op("tensor", bind(lambda e, tt=tt, ec=ec: e.matmul(
                    out=ps0[:, :], lhsT=kvbf[:, ec * 128:(ec + 1) * 128],
                    rhs=lqt[:, tt * 512:(tt + 1) * 512], start=True, stop=True)))
                s.dma("sync", bind(lambda e, tt=tt, ec=ec: e.dma_start(
                    out=qdt[:, :], in_=quad[ec * 128:(ec + 1) * 128, tt * 512:(tt + 1) * 512])))
                s.dma("sync", bind(lambda e, tt=tt, ec=ec: e.dma_start(
                    out=gtt[:, :], in_=gate[ec * 128:(ec + 1) * 128, tt * 512:(tt + 1) * 512])))
                s.op("vector", lambda e: e.tensor_tensor(out=scr[:, :], in0=ps0[:, :],
                                                         in1=qdt[:, :], op=ALU.add))
                s.op("vector", bind(lambda e, ec=ec: e.tensor_tensor(
                    out=outat[:, ec * 512:(ec + 1) * 512], in0=scr[:, :], in1=gtt[:, :],
                    op=ALU.mult)))
            for q in range(4):
                i = tt * 4 + q
                for dh in range(2):
                    for ec in range(16):
                        s.op("tensor", bind(lambda e, q=q, ec=ec, dh=dh: e.matmul(
                            out=ps1[:, :],
                            lhsT=outat[:, ec * 512 + q * 128: ec * 512 + (q + 1) * 128],
                            rhs=wob[:, ec * D + dh * 512: ec * D + (dh + 1) * 512],
                            start=(ec == 0), stop=False)))
                    s.op("tensor", bind(lambda e, dh=dh: e.matmul(
                        out=ps1[:, :], lhsT=ones1[0:1, :],
                        rhs=bort[0:1, dh * 512:(dh + 1) * 512], start=False, stop=True)))
                    s.dma("sync", bind(lambda e, i=i, dh=dh: e.dma_start(
                        out=lnxt[:, :], in_=lnx[i * 128:(i + 1) * 128, dh * 512:(dh + 1) * 512])))
                    s.dma("sync", bind(lambda e, i=i, dh=dh: e.dma_start(
                        out=x2mt[:, :], in_=x2m[i * 128:(i + 1) * 128, dh * 512:(dh + 1) * 512])))
                    s.op("vector", lambda e: e.tensor_tensor(out=x3t[:, :], in0=ps1[:, :],
                                                             in1=lnxt[:, :], op=ALU.add))
                    s.op("vector", lambda e: e.tensor_tensor(out=x3t[:, :], in0=x3t[:, :],
                                                             in1=x2mt[:, :], op=ALU.add))
                    s.dma("gpsimd", bind(lambda e, i=i, dh=dh: e.dma_start(
                        out=o_x3[i * 128:(i + 1) * 128, dh * 512:(dh + 1) * 512],
                        in_=x3t[:, :])))
        s.build(block, gsem, dsem)
    return nc


def prep_C2(inp, rb, rc1):
    kvp = np.stack([np.asarray(r["o_linkv"], np.float32) for r in rc1], axis=0)
    bfc = lambda a: np.ascontiguousarray(a).astype(ml_dtypes.bfloat16)
    common = {
        "kvp": kvp,
        "wout": np.asarray(inp["L1_attn_W_out"], np.float32),
        "bor": bfc(host_row(inp["L1_attn_b_out"])),
    }
    in_maps = []
    for c in range(NC):
        in_maps.append(dict(common,
                            x2m=np.ascontiguousarray(rb[c]["o_x2m"]),
                            lnx=np.ascontiguousarray(rc1[c]["o_lnx"]),
                            linq=np.ascontiguousarray(rc1[c]["o_linq"]),
                            quad=np.ascontiguousarray(rc1[c]["o_quad"]),
                            gate=np.ascontiguousarray(rc1[c]["o_gate"])))
    return in_maps


def run_C2(inp, rb, rc1):
    nc = build_C2()
    res = run_bass_kernel_spmd(nc, prep_C2(inp, rb, rc1), core_ids=CORE_IDS)
    return res.results



def build_D():
    nc = bass.Bass("TRN2", target_bir_lowering=False, debug=False)
    x3 = nc.dram_tensor("x3", [TL, D], F32, kind="ExternalInput").ap()
    xh = nc.dram_tensor("xh", [2, D], F32, kind="ExternalInput").ap()
    w1T = nc.dram_tensor("w1T", [D, 4096], F32, kind="ExternalInput").ap()
    w2T = nc.dram_tensor("w2T", [4096, D], F32, kind="ExternalInput").ap()
    cwc = nc.dram_tensor("cwc", [128, 32 * 3], F32, kind="ExternalInput").ap()
    cbc = nc.dram_tensor("cbc", [128, 32], F32, kind="ExternalInput").ap()
    idn = nc.dram_tensor("idn", [128, 128], F32, kind="ExternalInput").ap()
    o_out = nc.dram_tensor("o_out", [TL, D], F32, kind="ExternalOutput").ap()
    hsbuf = nc.dram_tensor("hsbuf", [4096, TL], F32, kind="Internal").ap()
    rsb = nc.dram_tensor("rsb", [16, 128], BF16, kind="Internal").ap()

    W = 2050
    s = Seq(nc)
    with ExitStack() as st:
        T_ = st.enter_context
        warena = T_(nc.sbuf_tensor("warena", [128, 32 * 1024], BF16))
        xrfm = T_(nc.sbuf_tensor("xrfm", [128, 8 * TL], BF16))
        xrhfm = T_(nc.sbuf_tensor("xrhfm", [128, 8 * 128], BF16))
        invsbc = T_(nc.sbuf_tensor("invsbc", [128, W], F32))
        s2bc = T_(nc.sbuf_tensor("s2bc", [128, TL], F32))
        hch = T_(nc.sbuf_tensor("hch", [128, W], F32))
        cvs = T_(nc.sbuf_tensor("cvs", [128, TL], F32))
        runmax = T_(nc.sbuf_tensor("runmaxd", [128, TL], F32))
        hr2 = T_(nc.sbuf_tensor("hr2", [128, 32 * 128], BF16))
        hslc = T_(nc.sbuf_tensor("hslc", [128, 32 * 128], F32))
        wload = T_(nc.sbuf_tensor("wloadd", [128, 1024], F32))
        xt = T_(nc.sbuf_tensor("xtd", [128, D], F32))
        lnt = T_(nc.sbuf_tensor("lntd", [128, D], F32))
        sq = T_(nc.sbuf_tensor("sqd2", [128, D], F32))
        xrtm = T_(nc.sbuf_tensor("xrtm", [128, D], BF16))
        stg = T_(nc.sbuf_tensor("stgd", [128, 128], BF16))
        cols = T_(nc.sbuf_tensor("colsd", [128, 8], F32))
        acc = T_(nc.sbuf_tensor("accd", [128, 4], F32))
        invw1 = T_(nc.sbuf_tensor("invw1", [128, 1], F32))
        invw2 = T_(nc.sbuf_tensor("invw2", [128, 1], F32))
        invh = T_(nc.sbuf_tensor("invh", [128, 1], F32))
        invhn = T_(nc.sbuf_tensor("invhn", [1, 1], F32))
        am2 = T_(nc.sbuf_tensor("am2", [128, 16], F32))
        s2c = T_(nc.sbuf_tensor("s2c", [128, 16], F32))
        i2c = T_(nc.sbuf_tensor("i2c", [128, 16], F32))
        rowt = T_(nc.sbuf_tensor("rowt", [16, 128], BF16))
        rowt1 = T_(nc.sbuf_tensor("rowt1", [1, 2048], BF16))
        sxb = T_(nc.sbuf_tensor("sxb", [128, 16], BF16))
        sxh = T_(nc.sbuf_tensor("sxh", [128, 1], BF16))
        sxhn = T_(nc.sbuf_tensor("sxhn", [1, 1], BF16))
        s2cb = T_(nc.sbuf_tensor("s2cb", [128, 16], BF16))
        idnb = T_(nc.sbuf_tensor("idnb", [128, 128], BF16))
        onesb1 = T_(nc.sbuf_tensor("onesb1", [1, 128], BF16))
        pstb = T_(nc.psum_tensor("pstbd", [128, 128], BF16))
        cwt = T_(nc.sbuf_tensor("cwt", [128, 32 * 3], F32))
        cbt = T_(nc.sbuf_tensor("cbt", [128, 32], F32))
        idnt = T_(nc.sbuf_tensor("idnt", [128, 128], F32))
        onesf = T_(nc.sbuf_tensor("onesf", [1, 128], F32))
        ones128 = T_(nc.sbuf_tensor("ones128d", [128, 1], F32))
        sc1 = T_(nc.sbuf_tensor("sc1", [1, 2], F32))
        epscol = T_(nc.sbuf_tensor("epscold", [128, 1], F32))
        scrh = T_(nc.sbuf_tensor("scrh", [128, 512], F32))
        ps0 = T_(nc.psum_tensor("ps0d", [128, 512], F32))
        ps1 = T_(nc.psum_tensor("ps1d", [128, 512], F32))
        psh = T_(nc.psum_tensor("pshd", [128, 128], F32))
        pst = T_(nc.psum_tensor("pstd", [128, 128], F32))
        psv = T_(nc.psum_tensor("psvd", [1, 1], F32))
        psb = T_(nc.psum_tensor("psbd", [128, 128], F32))
        gsem = T_(nc.semaphore("gsem"))
        dsem = T_(nc.semaphore("dsem"))
        block = T_(nc.Block())

        s.dma("sync", lambda e: e.dma_start(out=cwt[:, :], in_=cwc))
        s.dma("sync", lambda e: e.dma_start(out=cbt[:, :], in_=cbc))
        s.dma("sync", lambda e: e.dma_start(out=idnt[:, :], in_=idn))
        s.op("vector", lambda e: e.memset(onesf[:, :], 1.0))
        s.op("vector", lambda e: e.memset(ones128[:, :], 1.0))
        s.op("vector", lambda e: e.memset(runmax[:, :], 0.0))
        s.op("vector", lambda e: e.memset(epscol[:, :], 1e-5))
        s.op("vector", lambda e: e.memset(onesb1[:, :], 1.0))
        s.op("vector", lambda e: e.tensor_copy(out=idnb[:, :], in_=idnt[:, :]))

        def wquant(tiles, invdst):
            s.op("vector", lambda e: e.memset(acc[:, 0:1], 0.0))
            for (sl, _col) in tiles:
                s.dma("sync", bind(lambda e, sl=sl: e.dma_start(out=wload[:, :], in_=sl)))
                s.op("vector", lambda e: e.reduce_sum(out=acc[:, 1:2], in_=wload[:, :],
                                                      axis=AX.X, apply_absolute_value=True))
                s.op("vector", lambda e: e.tensor_tensor(out=acc[:, 0:1], in0=acc[:, 0:1],
                                                         in1=acc[:, 1:2], op=ALU.add))
            s.op("tensor", lambda e: e.matmul(out=psv[:, :], lhsT=acc[:, 0:1],
                                              rhs=ones128[:, :], start=True, stop=True))
            s.op("vector", lambda e: e.tensor_scalar(out=sc1[:, 0:1], in0=psv[:, :],
                                                     scalar1=1.0 / (4096.0 * 1024.0),
                                                     scalar2=1e-5, op0=ALU.mult, op1=ALU.max))
            s.op("vector", lambda e: e.reciprocal(out=sc1[:, 1:2], in_=sc1[:, 0:1]))
            s.op("tensor", lambda e: e.matmul(out=psb[:, 0:2], lhsT=onesf[:, :],
                                              rhs=sc1[:, 0:2], start=True, stop=True))
            s.op("vector", lambda e: e.tensor_copy(out=acc[:, 2:4], in_=psb[:, 0:2]))
            s.op("vector", bind(lambda e, invdst=invdst: e.tensor_copy(
                out=invdst[:, :], in_=acc[:, 2:3])))
            for (sl, col) in tiles:
                s.dma("sync", bind(lambda e, sl=sl: e.dma_start(out=wload[:, :], in_=sl)))
                s.op("vector", lambda e: e.tensor_scalar(out=wload[:, :], in0=wload[:, :],
                                                         scalar1=acc[:, 3:4], scalar2=MAGIC,
                                                         op0=ALU.mult, op1=ALU.add))
                s.op("vector", lambda e: e.tensor_scalar(out=wload[:, :], in0=wload[:, :],
                                                         scalar1=MAGIC, scalar2=-1.0,
                                                         op0=ALU.subtract, op1=ALU.max))
                s.op("vector", bind(lambda e, col=col: e.tensor_scalar(
                    out=warena[:, col:col + 1024], in0=wload[:, :],
                    scalar1=1.0, scalar2=None, op0=ALU.min)))

        w1_tiles = [(w1T[c * 128:(c + 1) * 128, m * 1024:(m + 1) * 1024], (c * 4 + m) * 1024)
                    for c in range(8) for m in range(4)]
        wquant(w1_tiles, invw1)

        def w1_lhsT(c, cc):
            m, off = cc // 8, (cc % 8) * 128
            base = (c * 4 + m) * 1024 + off
            return warena[:, base:base + 128]

        def ln_q():
            s.op("scalar", lambda e: e.activation(out=sq[:, :], in_=xt[:, :], func=AF.Square,
                                                  accum_out=cols[:, 0:1]))
            s.op("vector", lambda e: e.reduce_sum(out=cols[:, 1:2], in_=xt[:, :], axis=AX.X))
            s.op("vector", lambda e: e.tensor_scalar(out=cols[:, 2:3], in0=cols[:, 1:2],
                                                     scalar1=1.0 / D, scalar2=None, op0=ALU.mult))
            s.op("vector", lambda e: e.tensor_scalar(out=cols[:, 3:4], in0=cols[:, 0:1],
                                                     scalar1=1.0 / D, scalar2=None, op0=ALU.mult))
            s.op("vector", lambda e: e.scalar_tensor_tensor(out=cols[:, 4:5], in0=cols[:, 2:3],
                                                            scalar=cols[:, 2:3], in1=cols[:, 3:4],
                                                            op0=ALU.mult, op1=ALU.subtract))
            s.op("scalar", lambda e: e.activation(out=cols[:, 5:6], in_=cols[:, 4:5], func=AF.Sqrt,
                                                  scale=-1.0, bias=epscol[:, 0:1]))
            s.op("vector", lambda e: e.reciprocal(out=cols[:, 6:7], in_=cols[:, 5:6]))
            s.op("vector", lambda e: e.tensor_scalar(out=lnt[:, :], in0=xt[:, :],
                                                     scalar1=cols[:, 2:3], scalar2=cols[:, 6:7],
                                                     op0=ALU.subtract, op1=ALU.mult))
            s.op("vector", lambda e: e.reduce_max(out=cols[:, 0:1], in_=lnt[:, :], axis=AX.X,
                                                  apply_absolute_value=True))
            s.op("vector", lambda e: e.tensor_scalar(out=cols[:, 1:2], in0=cols[:, 0:1],
                                                     scalar1=1e-5, scalar2=None, op0=ALU.max))
            s.op("vector", lambda e: e.reciprocal(out=cols[:, 2:3], in_=cols[:, 1:2]))
            s.op("vector", lambda e: e.tensor_scalar(out=cols[:, 3:4], in0=cols[:, 2:3],
                                                     scalar1=127.0, scalar2=None, op0=ALU.mult))
            s.op("vector", lambda e: e.tensor_copy(out=sxh[:, 0:1], in_=cols[:, 3:4]))
            s.op("vector", lambda e: e.tensor_copy(out=cols[:, 3:4], in_=sxh[:, 0:1]))
            s.op("vector", lambda e: e.tensor_scalar(out=lnt[:, :], in0=lnt[:, :],
                                                     scalar1=cols[:, 3:4], scalar2=MAGIC,
                                                     op0=ALU.mult, op1=ALU.add))
            s.op("vector", lambda e: e.tensor_scalar(out=xrtm[:, :], in0=lnt[:, :],
                                                     scalar1=MAGIC, scalar2=None,
                                                     op0=ALU.subtract))

        for i in range(NT):
            s.dma("sync", bind(lambda e, i=i: e.dma_start(out=xt[:, :],
                                                          in_=x3[i * 128:(i + 1) * 128, :])))
            ln_q()
            s.op("vector", bind(lambda e, i=i: e.tensor_copy(out=sxb[:, i:i + 1],
                                                             in_=sxh[:, 0:1])))
            for j in range(8):
                s.dma("sync", bind(lambda e, i=i, j=j: e.dma_start_transpose(
                    out=xrfm[:, j * TL + i * 128: j * TL + (i + 1) * 128],
                    in_=xrtm[:, j * 128:(j + 1) * 128])))
        s.op("vector", lambda e: e.memset(xt[:, :], 0.0))
        s.dma("sync", lambda e: e.dma_start(out=xt[0:1, :], in_=xh[0:1, :]))
        s.dma("sync", lambda e: e.dma_start(out=xt[64:65, :], in_=xh[1:2, :]))
        ln_q()
        for j in range(8):
            s.dma("sync", bind(lambda e, j=j: e.dma_start_transpose(
                out=xrhfm[:, j * 128:(j + 1) * 128], in_=xrtm[:, j * 128:(j + 1) * 128])))

        s.op("tensor", lambda e: e.transpose(out=pstb[0:16, 0:128], in_=sxb[:, :],
                                             identity=idnb[:, :]))
        s.op("vector", lambda e: e.tensor_copy(out=rowt[:, :], in_=pstb[0:16, 0:128]))
        s.dma("gpsimd", lambda e: e.dma_start(out=rsb, in_=rowt[:, :]))
        s.dma("sync", lambda e: e.dma_start(out=rowt1[0:1, :],
                                            in_=rsb.rearrange("a b -> (a b)")[None, :]))
        for i in range(NT):
            s.op("tensor", bind(lambda e, i=i: e.matmul(
                out=psb[:, :], lhsT=onesb1[:, :], rhs=rowt1[0:1, i * 128:(i + 1) * 128],
                start=True, stop=True)))
            s.op("vector", bind(lambda e, i=i: e.tensor_copy(
                out=invsbc[:, 1 + i * 128: 1 + (i + 1) * 128], in_=psb[:, :])))
        s.op("tensor", lambda e: e.matmul(out=psb[:, 0:1], lhsT=onesb1[:, :],
                                          rhs=sxh[0:1, 0:1], start=True, stop=True))
        s.op("vector", lambda e: e.tensor_copy(out=invsbc[:, 0:1], in_=psb[:, 0:1]))
        s.dma("gpsimd", lambda e: e.dma_start(out=sxhn[0:1, 0:1], in_=sxh[64:65, 0:1]))
        s.op("tensor", lambda e: e.matmul(out=psb[:, 0:1], lhsT=onesb1[:, :],
                                          rhs=sxhn[0:1, 0:1], start=True, stop=True))
        s.op("vector", lambda e: e.tensor_copy(out=invsbc[:, 2049:2050], in_=psb[:, 0:1]))
        s.op("vector", lambda e: e.reciprocal(out=invsbc[:, :], in_=invsbc[:, :]))

        for cc in range(32):
            for tt in range(4):
                for c in range(8):
                    s.op("tensor", bind(lambda e, cc=cc, tt=tt, c=c: e.matmul(
                        out=ps0[:, :], lhsT=w1_lhsT(c, cc),
                        rhs=xrfm[:, c * TL + tt * 512: c * TL + (tt + 1) * 512],
                        start=(c == 0), stop=(c == 7))))
                s.op("vector", bind(lambda e, tt=tt: e.scalar_tensor_tensor(
                    out=hch[:, 1 + tt * 512: 1 + (tt + 1) * 512], in0=ps0[:, :],
                    scalar=invw1[:, 0:1],
                    in1=invsbc[:, 1 + tt * 512: 1 + (tt + 1) * 512],
                    op0=ALU.mult, op1=ALU.mult)))
            for c in range(8):
                s.op("tensor", bind(lambda e, cc=cc, c=c: e.matmul(
                    out=psh[:, :], lhsT=w1_lhsT(c, cc),
                    rhs=xrhfm[:, c * 128:(c + 1) * 128],
                    start=(c == 0), stop=(c == 7))))
            s.op("vector", lambda e: e.scalar_tensor_tensor(
                out=hch[:, 0:1], in0=psh[:, 0:1], scalar=invw1[:, 0:1],
                in1=invsbc[:, 0:1], op0=ALU.mult, op1=ALU.mult))
            s.op("vector", lambda e: e.scalar_tensor_tensor(
                out=hch[:, 2049:2050], in0=psh[:, 64:65], scalar=invw1[:, 0:1],
                in1=invsbc[:, 2049:2050], op0=ALU.mult, op1=ALU.mult))
            s.op("vector", bind(lambda e, cc=cc: e.tensor_scalar(
                out=cvs[:, :], in0=hch[:, 0:TL], scalar1=cwt[:, cc * 3:cc * 3 + 1],
                scalar2=None, op0=ALU.mult)))
            s.op("vector", bind(lambda e, cc=cc: e.scalar_tensor_tensor(
                out=cvs[:, :], in0=hch[:, 1:TL + 1], scalar=cwt[:, cc * 3 + 1:cc * 3 + 2],
                in1=cvs[:, :], op0=ALU.mult, op1=ALU.add)))
            s.op("vector", bind(lambda e, cc=cc: e.scalar_tensor_tensor(
                out=cvs[:, :], in0=hch[:, 2:TL + 2], scalar=cwt[:, cc * 3 + 2:cc * 3 + 3],
                in1=cvs[:, :], op0=ALU.mult, op1=ALU.add)))
            s.op("scalar", bind(lambda e, cc=cc: e.activation(
                out=cvs[:, :], in_=cvs[:, :], func=AF.Silu, bias=cbt[:, cc:cc + 1])))
            s.op("scalar", lambda e: e.activation(out=hch[:, 0:TL], in_=cvs[:, :], func=AF.Abs))
            s.op("vector", lambda e: e.tensor_tensor(out=runmax[:, :], in0=runmax[:, :],
                                                     in1=hch[:, 0:TL], op=ALU.max))
            s.dma("gpsimd", bind(lambda e, cc=cc: e.dma_start(
                out=hsbuf[cc * 128:(cc + 1) * 128, :], in_=cvs[:, :])))

        rmb = hr2
        for i in range(NT):
            s.op("vector", bind(lambda e, i=i: e.tensor_copy(
                out=rmb[:, 0:128], in_=runmax[:, i * 128:(i + 1) * 128])))
            s.op("tensor", bind(lambda e, i=i: e.transpose(
                out=pstb[:, :], in_=rmb[:, 0:128], identity=idnb[:, :])))
            s.op("vector", bind(lambda e, i=i: e.reduce_max(
                out=am2[:, i:i + 1], in_=pstb[:, :], axis=AX.X)))
        s.op("vector", lambda e: e.tensor_scalar(out=am2[:, :], in0=am2[:, :],
                                                 scalar1=1e-5, scalar2=None, op0=ALU.max))
        s.op("vector", lambda e: e.reciprocal(out=s2c[:, :], in_=am2[:, :]))
        s.op("vector", lambda e: e.tensor_scalar(out=s2c[:, :], in0=s2c[:, :],
                                                 scalar1=127.0, scalar2=None, op0=ALU.mult))
        s.op("vector", lambda e: e.tensor_copy(out=s2cb[:, :], in_=s2c[:, :]))
        s.op("vector", lambda e: e.reciprocal(out=i2c[:, :], in_=s2cb[:, :]))
        s.op("tensor", lambda e: e.transpose(out=pstb[0:16, 0:128], in_=s2cb[:, :],
                                             identity=idnb[:, :]))
        s.op("vector", lambda e: e.tensor_copy(out=rowt[:, :], in_=pstb[0:16, 0:128]))
        s.dma("gpsimd", lambda e: e.dma_start(out=rsb, in_=rowt[:, :]))
        s.dma("sync", lambda e: e.dma_start(out=rowt1[0:1, :],
                                            in_=rsb.rearrange("a b -> (a b)")[None, :]))
        for i in range(NT):
            s.op("tensor", bind(lambda e, i=i: e.matmul(
                out=psb[:, :], lhsT=onesb1[:, :], rhs=rowt1[0:1, i * 128:(i + 1) * 128],
                start=True, stop=True)))
            s.op("vector", bind(lambda e, i=i: e.tensor_copy(
                out=s2bc[:, i * 128:(i + 1) * 128], in_=psb[:, :])))

        w2_tiles = [(w2T[cc * 128:(cc + 1) * 128, :], cc * 1024) for cc in range(32)]
        wquant(w2_tiles, invw2)

        for i in range(NT):
            for cc in range(32):
                s.dma("sync", bind(lambda e, cc=cc, i=i: e.dma_start(
                    out=hslc[:, cc * 128:(cc + 1) * 128],
                    in_=hsbuf[cc * 128:(cc + 1) * 128, i * 128:(i + 1) * 128])))
                s.op("vector", bind(lambda e, cc=cc, i=i: e.tensor_tensor(
                    out=hslc[:, cc * 128:(cc + 1) * 128],
                    in0=hslc[:, cc * 128:(cc + 1) * 128],
                    in1=s2bc[:, i * 128:(i + 1) * 128], op=ALU.mult)))
                s.op("vector", bind(lambda e, cc=cc: e.tensor_scalar(
                    out=hr2[:, cc * 128:(cc + 1) * 128],
                    in0=hslc[:, cc * 128:(cc + 1) * 128],
                    scalar1=MAGIC, scalar2=MAGIC, op0=ALU.add, op1=ALU.subtract)))
            s.dma("sync", bind(lambda e, i=i: e.dma_start(out=xt[:, :],
                                                          in_=x3[i * 128:(i + 1) * 128, :])))
            for dh in range(2):
                for cc in range(32):
                    s.op("tensor", bind(lambda e, cc=cc, dh=dh: e.matmul(
                        out=ps1[:, :], lhsT=hr2[:, cc * 128:(cc + 1) * 128],
                        rhs=warena[:, cc * 1024 + dh * 512: cc * 1024 + (dh + 1) * 512],
                        start=(cc == 0), stop=(cc == 31))))
                s.op("vector", bind(lambda e, i=i: e.tensor_scalar(
                    out=scrh[:, :], in0=ps1[:, :], scalar1=i2c[:, i:i + 1],
                    scalar2=invw2[:, 0:1], op0=ALU.mult, op1=ALU.mult)))
                s.op("vector", bind(lambda e, dh=dh: e.tensor_tensor(
                    out=scrh[:, :], in0=scrh[:, :], in1=xt[:, dh * 512:(dh + 1) * 512],
                    op=ALU.add)))
                s.dma("gpsimd", bind(lambda e, i=i, dh=dh: e.dma_start(
                    out=o_out[i * 128:(i + 1) * 128, dh * 512:(dh + 1) * 512],
                    in_=scrh[:, :])))
        s.build(block, gsem, dsem)
    return nc


def prep_D(inp, rc2):
    cw = np.asarray(inp["L1_ffn_conv_w"], np.float32).reshape(4096, 3)
    cwc = np.ascontiguousarray(cw.reshape(32, 128, 3).transpose(1, 0, 2).reshape(128, 96))
    cbc = host_cols(inp["L1_ffn_conv_b"])
    tr = lambda k: np.ascontiguousarray(np.asarray(inp[k], np.float32).T)
    common = {
        "w1T": tr("L1_ffn_w1"), "w2T": tr("L1_ffn_w2"),
        "cwc": cwc, "cbc": cbc, "idn": np.eye(128, dtype=np.float32),
    }
    x3_full = np.concatenate([np.asarray(r["o_x3"], np.float32) for r in rc2], axis=0)
    in_maps = []
    for c in range(NC):
        prev = x3_full[c * TL - 1:c * TL] if c > 0 else np.zeros((1, D), np.float32)
        nxt = x3_full[(c + 1) * TL:(c + 1) * TL + 1] if c < NC - 1 else np.zeros((1, D), np.float32)
        in_maps.append(dict(common,
                            x3=np.ascontiguousarray(x3_full[c * TL:(c + 1) * TL]),
                            xh=np.ascontiguousarray(np.concatenate([prev, nxt], axis=0))))
    return in_maps


def run_D(inp, rc2):
    nc = build_D()
    res = run_bass_kernel_spmd(nc, prep_D(inp, rc2), core_ids=CORE_IDS)
    return res.results


def kernel(**inp):
    r1 = run_A1(inp)
    r2 = run_A2(r1)
    r3 = run_A3(inp, r1, r2)
    rb = run_B(inp, r3)
    rc1 = run_C1(inp, rb)
    rc2 = run_C2(inp, rb, rc1)
    rd = run_D(inp, rc2)
    out = np.concatenate([np.asarray(r["o_out"], np.float32) for r in rd], axis=0)
    return out.reshape(1, T, D)
